# version 11
# baseline (speedup 1.0000x reference)
import numpy as np
import ml_dtypes
import concourse.bass as bass
import concourse.mybir as mybir
from concourse.bass_utils import run_bass_kernel_spmd

F32 = mybir.dt.float32
BF16 = mybir.dt.bfloat16
AF = mybir.ActivationFunctionType
OP = mybir.AluOpType
AX = mybir.AxisListType
BFNP = ml_dtypes.bfloat16

TAPS = []
LAST_TAPS = {}
PS_STAGE = 99
KKVAR = 0


ENGS = ("T", "V", "S", "G", "Y")
LANES = {"T": 2, "V": 2, "S": 2, "G": 8, "Y": 16}


class Reg:
    __slots__ = ("w", "rs", "name")

    def __init__(self, name=""):
        self.w = None
        self.rs = []
        self.name = name


class Op:
    __slots__ = ("eng", "fn", "seq", "dma", "deps", "marked", "label", "dseq")

    def __init__(self, eng, fn, dma, label):
        self.eng = eng
        self.fn = fn
        self.dma = dma
        self.label = label
        self.deps = []
        self.marked = False
        self.seq = -1
        self.dseq = -1


class Prog:
    def __init__(self):
        self.ops = {e: [] for e in ENGS}
        self.ndma = {e: 0 for e in ENGS}

    def op(self, eng, fn, reads=(), writes=(), dma=False, label="", acc=False):
        o = Op(eng, fn, dma, label)
        raw = set()
        for r in reads:
            if r.w is not None:
                raw.add(r.w)
        oth = set()
        for r in writes:
            if r.w is not None:
                oth.add(r.w)
            oth.update(r.rs)
        deps = [d for d in raw if d is not o]
        for d in oth:
            if d is o or d in raw:
                continue
            if acc and (not d.dma) and d.eng == "T" and eng == "T":
                continue
            deps.append(d)
        o.deps = deps
        for r in reads:
            r.rs.append(o)
        for r in writes:
            r.w = o
            r.rs = []
        lst = self.ops[eng]
        o.seq = len(lst)
        lst.append(o)
        if dma:
            o.dseq = self.ndma[eng]
            self.ndma[eng] += 1
        return o

    def emit(self, nc, block, sems_exec, sems_dma):
        for e in ENGS:
            for o in self.ops[e]:
                for d in o.deps:
                    d.marked = True
        inc_rank = {}
        for e in ENGS:
            c = 0
            for o in self.ops[e]:
                if o.dma:
                    continue
                if o.marked:
                    c += 1
                    inc_rank[o] = c
        dep_val_cache = {}

        def dep_token(d):
            if d in dep_val_cache:
                return dep_val_cache[d]
            if d.dma:
                R = LANES[d.eng]
                tok = (("D", d.eng, d.dseq % R), 16 * (d.dseq // R + 1))
            else:
                tok = (("X", d.eng, 0), inc_rank[d])
            dep_val_cache[d] = tok
            return tok

        def make_section(e):
            ops = self.ops[e]

            def section(engobj):
                hw = {}
                for o in ops:
                    waits = {}
                    for d in o.deps:
                        key, val = dep_token(d)
                        if waits.get(key, 0) < val:
                            waits[key] = val
                    if o.dma:
                        R = LANES[e]
                        lane, pos = o.dseq % R, o.dseq // R
                        if pos >= 1:
                            key = ("D", e, lane)
                            if waits.get(key, 0) < 16 * pos:
                                waits[key] = 16 * pos
                    for (kind, de, lane), val in sorted(waits.items()):
                        if hw.get((kind, de, lane), 0) >= val:
                            continue
                        hw[(kind, de, lane)] = val
                        sem = sems_dma[de][lane] if kind == "D" else sems_exec[de]
                        engobj.wait_ge(sem, val)
                    inst = o.fn(engobj)
                    if o.dma:
                        inst.then_inc(sems_dma[e][o.dseq % LANES[e]], 16)
                    elif o.marked:
                        inst.then_inc(sems_exec[e], 1)

            return section

        block.tensor(make_section("T"))
        block.vector(make_section("V"))
        block.scalar(make_section("S"))
        block.gpsimd(make_section("G"))
        block.sync(make_section("Y"))


class Arena:

    def __init__(self, P, tile, ncols, name):
        self.P = P
        self.tile = tile
        self.ncols = ncols
        self.name = name
        self.ptr = 0
        self.spans = []

    def reset(self):
        self.ptr = 0

    def alloc(self, name, ncols):
        s = self.ptr
        e = s + ncols
        assert e <= self.ncols, f"arena {self.name} overflow at {name}: {e} > {self.ncols}"
        self.ptr = e
        return self._mkspan(name, s, e)

    def alloc_at(self, name, ncols, start):
        s, e = start, start + ncols
        assert e <= self.ncols, f"arena {self.name} overflow at {name}: {e} > {self.ncols}"
        return self._mkspan(name, s, e)

    def _mkspan(self, name, s, e):
        reg = Reg(name)
        newspans = []
        for (a, b, rl) in self.spans:
            if b <= s or a >= e:
                newspans.append((a, b, rl))
            else:
                for r in rl:
                    if r.w is not None:
                        reg.rs.append(r.w)
                    reg.rs.extend(r.rs)
                if a < s:
                    newspans.append((a, s, rl))
                if b > e:
                    newspans.append((e, b, rl))
        newspans.append((s, e, [reg]))
        self.spans = newspans
        return self.tile[:, s:e], reg

    def bind(self, base, regs):
        for i, (s, e, rl) in enumerate(self.spans):
            if base in rl:
                self.spans[i] = (s, e, list(rl) + list(regs))
                return
        raise KeyError("bind: base reg not found in any span")

    def allocp(self, name, ncols, nparts):
        ap, reg = self.alloc(name, ncols)
        return ap[0:nparts, :], reg



def chunk_lhsT(w_od):
    o, d = w_od.shape
    assert d % 128 == 0
    wt = np.ascontiguousarray(
        w_od.T.reshape(d // 128, 128, o).transpose(1, 0, 2).reshape(128, -1))
    return wt.astype(np.float32, copy=False)


def chunk_rhs(w_do):
    d, o = w_do.shape
    assert d % 128 == 0
    wt = np.ascontiguousarray(
        w_do.reshape(d // 128, 128, o).transpose(1, 0, 2).reshape(128, -1))
    return wt.astype(np.float32, copy=False)


def col_chunks(v):
    return np.ascontiguousarray(v.reshape(-1, 128).T).astype(np.float32, copy=False)


def host_consts():
    i = np.arange(128)
    c = {}
    c["c_biases"] = np.tile(np.array([[1e-6, 1e-5, 1.0]], np.float32), (128, 1))
    c["c_ident_bf"] = np.eye(128, dtype=np.float32).astype(BFNP)
    c["c_ident32"] = np.eye(128, dtype=np.float32)
    c["c_ones1"] = np.ones((1, 128), np.float32)
    c["c_ones128"] = np.ones((128, 1), np.float32)
    c["c_ones128b"] = np.ones((128, 1), np.float32).astype(BFNP)
    c["c_lowtri"] = (i[:, None] <= i[None, :]).astype(np.float32)
    KRAMP = 60000.0
    up = -KRAMP * (i[:, None] > i[None, :]).astype(np.float32)
    c["c_upneg_ext"] = np.concatenate([up, np.zeros((128, 1), np.float32)], axis=1)
    ie = np.concatenate([np.eye(128, dtype=np.float32),
                         (i == 127).astype(np.float32)[:, None]], axis=1)
    c["c_ident_ext"] = ie
    c["c_ImOne"] = (np.eye(128, dtype=np.float32) - 1.0)
    ob = np.zeros((128, 8 * 16), np.float32)
    sel = np.zeros((16, 8 * 128), np.float32)
    for k in range(8):
        for half in range(2):
            h = 2 * k + half
            ob[64 * half:64 * half + 64, 16 * k + h] = 1.0
            sel[h, 128 * k + 64 * half: 128 * k + 64 * half + 64] = 1.0
    id64 = np.eye(64, dtype=np.float32)
    c["c_ident2"] = np.vstack([id64, id64]).astype(BFNP)
    c["c_onesblk"] = ob.astype(BFNP)
    c["c_onesblk32"] = ob
    c["c_selT"] = sel
    return c



HID = 1024
NCH = 8
H = 16
DK = 64
DV = 128
CC = 128
TT4 = 512


def build(T=2048, tap_names=(), phases=("PA", "PS", "PA2", "PB", "PC", "PD")):
    NT = T // CC
    NTT = T // TT4
    nc = bass.Bass("TRN2", target_bir_lowering=False)
    taps = {}

    def dram_in(name, shape, dt=F32):
        return nc.dram_tensor(name, list(shape), dt, kind="ExternalInput").ap()

    xin = dram_in("x", (128, NCH, T))
    d_nw = dram_in("nw", (128, 8))
    d_qw = dram_in("qw", (128, 8 * 1024))
    d_kw = dram_in("kw", (128, 8 * 1024))
    d_vw = dram_in("vw", (128, 8 * 2048))
    d_abw = dram_in("abw", (128, 8 * 48))
    d_gw = dram_in("gw", (128, 8 * 2048))
    d_opw = dram_in("opw", (128, 16 * 1024))
    d_outw = dram_in("outw", (128, 8 * 1024))
    d_convq = dram_in("convq", (128, 8 * 4))
    d_convk = dram_in("convk", (128, 8 * 4))
    d_convv = dram_in("convv", (128, 16 * 4))
    d_dtb = dram_in("dtb", (16, 1))
    d_alog = dram_in("alog", (16, 1))
    d_onw = dram_in("onw", (128, 1))
    d_ffw = dram_in("ffw", (128, 8 * 1024))
    d_hpm = dram_in("hpm", (128, 8 * 1024), BF16)
    d_wq1 = dram_in("wq1", (128, 8 * 1024))
    d_wk1 = dram_in("wk1", (128, 8 * 1024))
    d_wv1 = dram_in("wv1", (128, 8 * 1024))
    d_bq1 = dram_in("bq1", (128, 8))
    d_bk1 = dram_in("bk1", (128, 8))
    d_bv1 = dram_in("bv1", (128, 8))
    d_wg1 = dram_in("wg1", (128, 8 * 4096))
    d_wu1 = dram_in("wu1", (128, 8 * 4096))
    d_wd1 = dram_in("wd1", (128, 32 * 1024))
    d_maskmid = dram_in("maskmid", (128, 384))
    d_maskfirst = dram_in("maskfirst", (128, 256))
    d_masklast = dram_in("masklast", (128, 256))
    CN = host_consts()
    d_c = {k: dram_in(k, v.shape, BF16 if v.dtype == BFNP else F32)
           for k, v in CN.items()}

    yout = nc.dram_tensor("y", [128, NCH, T], F32, kind="ExternalOutput").ap()

    def scr(name, shape, dt=BF16):
        return nc.dram_tensor(name, list(shape), dt, kind="Internal").ap()

    s_xn = scr("s_xn", (128, NCH, T))
    s_q = scr("s_q", (128, NT, 8, 128))
    s_k = scr("s_k", (128, NT, 8, 128))
    s_kb = scr("s_kb", (128, NT, 8, 128))
    s_v = scr("s_v", (128, NT, 16, 128))
    s_o = scr("s_o", (128, NT, 16, 128))
    s_x2 = scr("s_x2", (128, NCH, T))
    s_x3 = scr("s_x3", (128, NCH, T))
    s_x4 = scr("s_x4", (128, NCH, T))
    s_tg = scr("s_tg", (128, 8, 4096))
    s_tu = scr("s_tu", (128, 8, 4096))
    s_td = scr("s_td", (128, 32, 1024))

    sr_xn = [Reg(f"sxn{t}") for t in range(NTT)]
    sr_q = [[Reg(f"ssq{t}_{j}") for j in range(8)] for t in range(NTT)]
    sr_k = [[Reg(f"ssk{t}_{j}") for j in range(8)] for t in range(NTT)]
    sr_kb = [[Reg(f"sskb{t}_{j}") for j in range(8)] for t in range(NTT)]
    sr_v = [[Reg(f"ssv{t}_{j}") for j in range(16)] for t in range(NTT)]
    sr_o = [Reg(f"sso{c}") for c in range(NT)]
    sr_x2 = [Reg(f"sx2{t}") for t in range(NTT)]
    sr_x3 = [Reg(f"sx3{t}") for t in range(NTT)]
    sr_x4 = [Reg(f"sx4{c}") for c in range(NT)]
    sr_w = [[Reg(f"sw{w}_{p}") for p in range(16)] for w in range(3)]

    P = Prog()
    import contextlib
    ctx = contextlib.ExitStack()

    def sb(name, shape, dt=BF16):
        return ctx.enter_context(nc.sbuf_tensor(name + "_sb", list(shape), dt))

    def ps(name, shape, dt=F32):
        return ctx.enter_context(nc.psum_tensor(name, list(shape), dt))

    AB_COLS = 73728
    AF_COLS = 6912
    GW_OFS = AB_COLS - 32768
    ab_t = sb("AB", (128, AB_COLS), BF16)
    af_t = sb("AFa", (128, AF_COLS), F32)
    AB = Arena(P, ab_t, AB_COLS, "AB")
    AFA = Arena(P, af_t, AF_COLS, "AF")

    def bindsplit(base, regs):
        for A in (AB, AFA):
            try:
                A.bind(base, regs)
                return
            except KeyError:
                pass
        raise KeyError("bindsplit: base not found")

    gb_t = sb("gb", (64, T), F32); gb_r = Reg("gb")
    nw_t = sb("nw", (128, 8), F32); nw_r = Reg("nw")
    convs_t = sb("convs", (128, (8 + 8 + 16) * 4), F32); convs_r = Reg("convs")
    dtb_t = sb("dtb", (16, 1), F32); dtb_r = Reg("dtb")
    alog_t = sb("alog", (16, 1), F32); alog_r = Reg("alog")
    negea_t = sb("negea", (16, 1), F32); negea_r = Reg("negea")
    onw_t = sb("onw", (128, 1), F32); onw_r = Reg("onw")
    rows_t = sb("rows", (16, 2 * TT4), F32)
    normrow_t = rows_t[:, 0:TT4]; normrow_r = Reg("normrow")
    norm2row_t = rows_t[:, TT4:2 * TT4]; norm2row_r = Reg("norm2row")
    row1_t = sb("row1", (1, 2 * TT4 + 2048), F32)
    rowA = row1_t[0:1, 0:TT4]; rowA_r = Reg("rowA")
    rowB = row1_t[0:1, TT4:2 * TT4]; rowB_r = Reg("rowB")
    rowC = row1_t[0:1, 2 * TT4:2 * TT4 + 1024]; rowC_r = Reg("rowC")
    rowD = row1_t[0:1, 2 * TT4 + 1024:2 * TT4 + 2048]; rowD_r = Reg("rowD")
    col_t = sb("cols", (128, 16), F32)
    col_r = [Reg(f"col{j}") for j in range(16)]
    masks_t = {"mid": sb("maskmid_s", (128, 384), F32),
               "first": sb("maskfirst_s", (128, 256), F32),
               "last": sb("masklast_s", (128, 256), F32)}
    masks_r = {k: Reg("mask" + k) for k in masks_t}
    cn_t = {}
    cn_r = {}
    for k, v in CN.items():
        cn_t[k] = sb(k + "_s", v.shape, BF16 if v.dtype == BFNP else F32)
        cn_r[k] = Reg(k)

    pbank = [ps(f"pb{i}", (128, 512), F32) for i in range(6)]
    pbank_r = [Reg(f"pb{i}") for i in range(6)]
    pbf = [ps(f"pbf{i}", (128, 1024), BF16) for i in range(2)]
    pbf_r = [Reg(f"pbf{i}") for i in range(2)]

    def dma(eng, dst_ap, src_ap, writes, reads=(), label=""):
        return P.op(eng, lambda E, d=dst_ap, s=src_ap: E.dma_start(out=d, in_=s),
                    reads=reads, writes=writes, dma=True, label=label)

    def tap(name, src_ap, shape, reads, dt=F32):
        if name not in tap_names:
            return
        tp = nc.dram_tensor("tap_" + name, list(shape), dt, kind="ExternalOutput").ap()
        dma("Y", tp[:, :], src_ap, [], reads=reads, label="tap_" + name)
        taps[name] = shape

    dma("Y", nw_t[:], d_nw[:, :], [nw_r])
    dma("Y", convs_t[:, 0:32], d_convq[:, :], [convs_r])
    dma("Y", convs_t[:, 32:64], d_convk[:, :], [convs_r])
    dma("Y", convs_t[:, 64:128], d_convv[:, :], [convs_r])
    dma("Y", dtb_t[:], d_dtb[:, :], [dtb_r])
    dma("Y", alog_t[:], d_alog[:, :], [alog_r])
    dma("Y", onw_t[:], d_onw[:, :], [onw_r])
    for k in CN:
        dma("Y", cn_t[k][:, :], d_c[k][:, :], [cn_r[k]])
    nc.const_aps.register(F32, 1e-6, cn_t["c_biases"][:, 0:1])
    nc.const_aps.register(F32, 1e-5, cn_t["c_biases"][:, 1:2])
    nc.const_aps.register(F32, 1.0, cn_t["c_biases"][:, 2:3])
    cbias_r = cn_r["c_biases"]
    dma("Y", masks_t["mid"][:], d_maskmid[:, :], [masks_r["mid"]])
    dma("Y", masks_t["first"][:], d_maskfirst[:, :], [masks_r["first"]])
    dma("Y", masks_t["last"][:], d_masklast[:, :], [masks_r["last"]])

    P.op("S", lambda E: E.memzero(gb_t[0:64, :]), writes=[gb_r], label="gbz")

    P.op("S", lambda E: E.activation(negea_t[:], alog_t[:], AF.Exp),
         reads=[alog_r], writes=[negea_r])
    P.op("V", lambda E: E.tensor_scalar(negea_t[:], negea_t[:], -1.0, None, OP.mult),
         reads=[negea_r], writes=[negea_r])

    M23 = 1.5 * 2.0 ** 23
    M52 = 1.5 * 2.0 ** 52
    selc_t = sb("selc", (128, 2), F32); selc_r = Reg("selc")
    P.op("V", lambda E: E.tensor_scalar(
        selc_t[:, 0:1], cn_t["c_biases"][:, 2:3], 2.0 ** -25, -1.0, OP.add, OP.add),
        reads=[cn_r["c_biases"]], writes=[selc_r])
    P.op("V", lambda E: E.tensor_scalar(
        selc_t[:, 0:1], selc_t[:, 0:1], 2.0 ** 25, None, OP.mult),
        reads=[selc_r], writes=[selc_r])
    P.op("V", lambda E: E.tensor_scalar(
        selc_t[:, 1:2], selc_t[:, 0:1], -1.0, 1.0, OP.mult, OP.add),
        reads=[selc_r], writes=[selc_r])

    def emit_ln(W, src_ap, src_r, dst_ap, dst_r, bank0, bank1, sqa, sq_r, extra_out=None):
        okey = "c_ones128b" if src_ap.tensor.dtype == BF16 else "c_ones128"
        for k in range(NCH):
            P.op("T", lambda E, k=k: E.matmul(
                pbank[bank0][0:1, 0:W], cn_t[okey][:, 0:1],
                src_ap[:, k * W:(k + 1) * W], start=(k == 0), stop=(k == NCH - 1)),
                reads=[src_r, cn_r[okey]], writes=[pbank_r[bank0]], label=f"lnsum{k}", acc=(k != 0))
        P.op("V", lambda E: E.tensor_scalar(rowA[:, 0:W], pbank[bank0][0:1, 0:W],
                                            1.0 / 1024.0, None, OP.mult),
             reads=[pbank_r[bank0]], writes=[rowA_r], label="lnmean")
        for k in range(NCH):
            P.op("S", lambda E, k=k: E.activation(sqa[:, 0:W], src_ap[:, k * W:(k + 1) * W],
                                                  AF.Square),
                 reads=[src_r], writes=[sq_r], label=f"lnsq{k}")
            P.op("T", lambda E, k=k: E.matmul(
                pbank[bank0][0:1, 0:W], cn_t["c_ones128"][:, 0:1], sqa[:, 0:W],
                start=(k == 0), stop=(k == NCH - 1)),
                reads=[sq_r, cn_r["c_ones128"]], writes=[pbank_r[bank0]], label=f"lnss{k}", acc=(k != 0))
        P.op("V", lambda E: E.tensor_tensor(rowB[:, 0:W], rowA[:, 0:W], rowA[:, 0:W],
                                            OP.mult),
             reads=[rowA_r], writes=[rowB_r])
        P.op("V", lambda E: E.scalar_tensor_tensor(
            rowB[:, 0:W], pbank[bank0][0:1, 0:W], 1.0 / 1024.0, rowB[:, 0:W],
            OP.mult, OP.subtract),
            reads=[pbank_r[bank0], rowB_r], writes=[rowB_r], label="lnvar")
        P.op("S", lambda E: E.activation(rowB[:, 0:W], rowB[:, 0:W], AF.Sqrt, bias=1e-5),
             reads=[rowB_r, cbias_r], writes=[rowB_r], label="lnstd")
        P.op("V", lambda E: E.reciprocal(rowB[:, 0:W], rowB[:, 0:W]),
             reads=[rowB_r], writes=[rowB_r], label="lnrstd")
        P.op("T", lambda E: E.matmul(pbank[bank0][:, 0:W], cn_t["c_ones1"][0:1, :],
                                     rowA[:, 0:W], start=True, stop=True),
             reads=[rowA_r, cn_r["c_ones1"]], writes=[pbank_r[bank0]], label="lnbm")
        P.op("T", lambda E: E.matmul(pbank[bank1][:, 0:W], cn_t["c_ones1"][0:1, :],
                                     rowB[:, 0:W], start=True, stop=True),
             reads=[rowB_r, cn_r["c_ones1"]], writes=[pbank_r[bank1]], label="lnbr")
        for k in range(NCH):
            P.op("V", lambda E, k=k: E.tensor_tensor(
                src_ap[:, k * W:(k + 1) * W], src_ap[:, k * W:(k + 1) * W],
                pbank[bank0][:, 0:W], OP.subtract),
                reads=[src_r, pbank_r[bank0]], writes=[src_r], label=f"lnsub{k}")
            P.op("V", lambda E, k=k: E.tensor_tensor(
                dst_ap[:, k * W:(k + 1) * W], src_ap[:, k * W:(k + 1) * W],
                pbank[bank1][:, 0:W], OP.mult),
                reads=[src_r, pbank_r[bank1]], writes=[dst_r], label=f"lnmul{k}")
            if extra_out is not None:
                eap, er = extra_out
                P.op("V", lambda E, k=k: E.tensor_tensor(
                    eap[:, k * W:(k + 1) * W], src_ap[:, k * W:(k + 1) * W],
                    pbank[bank1][:, 0:W], OP.mult),
                    reads=[src_r, pbank_r[bank1]], writes=[er], label=f"lnmulf{k}")

    if "PA" in phases:
        AB.reset(); AFA.reset()
        qw_t, qw_r = AB.alloc("qw", 8 * 1024)
        kw_t, kw_r = AB.alloc("kw", 8 * 1024)
        vw_t, vw_r = AB.alloc("vw", 8 * 2048)
        abw_t, abw_r = AB.alloc("abw", 8 * 48)
        xn_t, xn_r0 = AB.alloc("xn", 8 * TT4)
        xn_r = [Reg(f"xn{k}") for k in range(8)]
        for r in xn_r:
            r.rs = list(xn_r0.rs)
        bindsplit(xn_r0, xn_r)
        qpost_t, _q0 = AB.alloc("qpost", 8 * TT4)
        kpost_t, _k0 = AB.alloc("kpost", 8 * TT4)
        vpost_t, _v0 = AB.alloc("vpost", 16 * TT4)
        kbeta_t, _kb0 = AB.alloc("kbeta", 8 * TT4)
        qpost_r = [Reg(f"q{j}") for j in range(8)]
        kpost_r = [Reg(f"k{j}") for j in range(8)]
        vpost_r = [Reg(f"v{j}") for j in range(16)]
        kbeta_r = [Reg(f"kb{j}") for j in range(8)]
        for rs, base in ((qpost_r, _q0), (kpost_r, _k0), (vpost_r, _v0), (kbeta_r, _kb0)):
            for r in rs:
                r.rs = list(base.rs)
            bindsplit(base, rs)

        xt_t, xt_r = AFA.alloc("xtile", 8 * TT4 // 1)
        sq_t, sq_r = AFA.alloc("sq", TT4)
        pre_t, _p0 = AFA.alloc("pre", 2 * (TT4 + 3 + 1))
        pre_r = [Reg("pre0"), Reg("pre1")]
        for r in pre_r:
            r.rs = list(_p0.rs)
        bindsplit(_p0, pre_r)
        tails_t, _t0 = AFA.alloc("tails", 32 * 3)
        tails_r = [Reg(f"tail{j}") for j in range(32)]
        for r in tails_r:
            r.rs = list(_t0.rs)
        bindsplit(_t0, tails_r)
        tmpA_t, tmpA_r = AFA.alloc("tmpA", TT4)
        tmpB_t, tmpB_r = AFA.alloc("tmpB", TT4)

        dma("G", qw_t[:, :], d_qw[:, :], [qw_r])
        dma("G", kw_t[:, :], d_kw[:, :], [kw_r])
        dma("G", vw_t[:, :], d_vw[:, :], [vw_r])
        dma("G", abw_t[:, :], d_abw[:, :], [abw_r])
        P.op("V", lambda E: E.memset(tails_t[:, :], 0.0), writes=tails_r, label="tails0")

        def emit_xn(tt, bank):
            ts = tt * TT4
            dma("Y", xt_t[:, :], xin[:, :, ts:ts + TT4], [xt_r], label=f"ldx{tt}")
            for k in range(NCH):
                P.op("S", lambda E, k=k: E.activation(
                    sq_t[:, :], xt_t[:, k * TT4:(k + 1) * TT4], AF.Square),
                    reads=[xt_r], writes=[sq_r], label=f"rsq{k}")
                P.op("T", lambda E, k=k, bank=bank: E.matmul(
                    pbank[bank][0:1, 0:TT4], cn_t["c_ones128"][:, 0:1], sq_t[:, :],
                    start=(k == 0), stop=(k == NCH - 1)),
                    reads=[sq_r, cn_r["c_ones128"]], writes=[pbank_r[bank]], label=f"rmm{k}", acc=(k != 0))
            P.op("S", lambda E, bank=bank: E.activation(
                rowB[:, :], pbank[bank][0:1, 0:TT4], AF.Sqrt,
                bias=1e-6, scale=1.0 / 1024.0),
                reads=[pbank_r[bank], cbias_r], writes=[rowB_r], label="rms_std")
            P.op("V", lambda E: E.reciprocal(rowA[:, :], rowB[:, :]),
                 reads=[rowB_r], writes=[rowA_r], label="rms_rstd")
            P.op("T", lambda E, bank=bank: E.matmul(
                pbank[bank][:, 0:TT4], cn_t["c_ones1"][0:1, :], rowA[:, :],
                start=True, stop=True),
                reads=[rowA_r, cn_r["c_ones1"]], writes=[pbank_r[bank]], label=f"rbc{tt}")
            for k in range(NCH):
                P.op("V", lambda E, k=k, bank=bank: E.scalar_tensor_tensor(
                    xn_t[:, k * TT4:(k + 1) * TT4], xt_t[:, k * TT4:(k + 1) * TT4],
                    nw_t[:, k:k + 1], pbank[bank][:, 0:TT4], OP.mult, OP.mult),
                    reads=[xt_r, nw_r, pbank_r[bank]], writes=[xn_r[k]], label=f"xn{tt}_{k}")
            dma("Y", s_xn[:, :, ts:ts + TT4], xn_t[:, :],
                [sr_xn[tt]], reads=xn_r, label=f"stxn{tt}")

        def proj_conv(tt, wt, wr, osize, conv_ofs, tail_base, post_t, post_r, bank):
            for j in range(osize):
                for k in range(NCH):
                    P.op("T", lambda E, j=j, k=k, bank=bank: E.matmul(
                        pbank[bank][:, 0:TT4],
                        wt[:, k * (osize * 128) + j * 128: k * (osize * 128) + j * 128 + 128],
                        xn_t[:, k * TT4:(k + 1) * TT4],
                        start=(k == 0), stop=(k == NCH - 1)),
                        reads=[wr, xn_r[k]], writes=[pbank_r[bank]], label=f"pj{tail_base}_{j}_{k}", acc=(k != 0))
                jj = tail_base + j
                pr = pre_r[j % 2]
                pre_ap = pre_t[:, (j % 2) * (TT4 + 4):(j % 2) * (TT4 + 4) + TT4 + 3]
                P.op("S", lambda E, pre_ap=pre_ap, bank=bank: E.activation(
                    pre_ap[:, 3:3 + TT4], pbank[bank][:, 0:TT4], AF.Copy),
                    reads=[pbank_r[bank]], writes=[pr], label=f"ev{jj}")
                P.op("V", lambda E, pre_ap=pre_ap, jj=jj: E.tensor_copy(
                    pre_ap[:, 0:3], tails_t[:, jj * 3:jj * 3 + 3]),
                    reads=[tails_r[jj]], writes=[pr], label=f"tl{jj}")
                wc = convs_t[:, conv_ofs + j * 4: conv_ofs + j * 4 + 4]
                P.op("V", lambda E, pre_ap=pre_ap, wc=wc: E.tensor_scalar(
                    tmpA_t[:, :], pre_ap[:, 0:TT4], wc[:, 0:1], None, OP.mult),
                    reads=[pr, convs_r], writes=[tmpA_r], label=f"cv0_{jj}")
                P.op("V", lambda E, pre_ap=pre_ap, wc=wc: E.scalar_tensor_tensor(
                    tmpB_t[:, :], pre_ap[:, 1:1 + TT4], wc[:, 1:2], tmpA_t[:, :],
                    OP.mult, OP.add),
                    reads=[pr, convs_r, tmpA_r], writes=[tmpB_r], label=f"cv1_{jj}")
                P.op("V", lambda E, pre_ap=pre_ap, wc=wc: E.scalar_tensor_tensor(
                    tmpA_t[:, :], pre_ap[:, 2:2 + TT4], wc[:, 2:3], tmpB_t[:, :],
                    OP.mult, OP.add),
                    reads=[pr, convs_r, tmpB_r], writes=[tmpA_r], label=f"cv2_{jj}")
                P.op("V", lambda E, pre_ap=pre_ap, wc=wc: E.scalar_tensor_tensor(
                    tmpB_t[:, :], pre_ap[:, 3:3 + TT4], wc[:, 3:4], tmpA_t[:, :],
                    OP.mult, OP.add),
                    reads=[pr, convs_r, tmpA_r], writes=[tmpB_r], label=f"cv3_{jj}")
                P.op("V", lambda E, pre_ap=pre_ap, jj=jj: E.tensor_copy(
                    tails_t[:, jj * 3:jj * 3 + 3], pre_ap[:, TT4:TT4 + 3]),
                    reads=[pr], writes=[tails_r[jj]], label=f"tls{jj}")
                P.op("S", lambda E, j=j: E.activation(
                    post_t[:, j * TT4:(j + 1) * TT4], tmpB_t[:, :], AF.Silu),
                    reads=[tmpB_r], writes=[post_r[j]], label=f"si{jj}")

        def emit_norms(tt, post_t, post_r, is_q, bankS, bankB):
            ob = cn_t["c_onesblk"]; obr = cn_r["c_onesblk"]
            selT = cn_t["c_selT"]; selr = cn_r["c_selT"]
            if not is_q:
                for k in range(8):
                    P.op("T", lambda E, k=k: E.matmul(
                        pbank[bankS][0:16, 0:TT4], ob[:, 16 * k:16 * (k + 1)],
                        post_t[:, k * TT4:(k + 1) * TT4],
                        start=(k == 0), stop=(k == 7)),
                        reads=[post_r[k], obr], writes=[pbank_r[bankS]], label=f"ks{k}", acc=(k != 0))
                P.op("V", lambda E: E.tensor_scalar(
                    normrow_t[:, :], pbank[bankS][0:16, 0:TT4], 1.0 / 64.0, None, OP.mult),
                    reads=[pbank_r[bankS]], writes=[normrow_r], label="kmean")
                for k in range(8):
                    P.op("T", lambda E, k=k: E.matmul(
                        pbank[bankB][:, 0:TT4], selT[0:16, 128 * k:128 * (k + 1)],
                        normrow_t[:, :], start=True, stop=True),
                        reads=[normrow_r, selr], writes=[pbank_r[bankB]], label=f"kmb{k}")
                    P.op("V", lambda E, k=k: E.tensor_tensor(
                        post_t[:, k * TT4:(k + 1) * TT4], post_t[:, k * TT4:(k + 1) * TT4],
                        pbank[bankB][:, 0:TT4], OP.subtract),
                        reads=[post_r[k], pbank_r[bankB]], writes=[post_r[k]], label=f"ksb{k}")
            for k in range(8):
                P.op("V", lambda E, k=k: E.tensor_tensor(
                    sq_t[:, :], post_t[:, k * TT4:(k + 1) * TT4],
                    post_t[:, k * TT4:(k + 1) * TT4], OP.mult),
                    reads=[post_r[k]], writes=[sq_r], label=f"nq{k}")
                P.op("T", lambda E, k=k: E.matmul(
                    pbank[bankS][0:16, 0:TT4],
                    cn_t["c_onesblk32"][:, 16 * k:16 * (k + 1)], sq_t[:, :],
                    start=(k == 0), stop=(k == 7)),
                    reads=[sq_r, cn_r["c_onesblk32"]], writes=[pbank_r[bankS]],
                    label=f"nss{k}", acc=(k != 0))
            scl = 0.125 if is_q else 1.0
            P.op("S", lambda E, scl=scl: E.activation(
                norm2row_t[:, :], pbank[bankS][0:16, 0:TT4], AF.Sqrt, bias=1e-6),
                reads=[pbank_r[bankS], cbias_r], writes=[norm2row_r], label="nsq")
            P.op("V", lambda E: E.reciprocal(norm2row_t[:, :], norm2row_t[:, :]),
                 reads=[norm2row_r], writes=[norm2row_r], label="nrsq")
            if is_q:
                P.op("V", lambda E: E.tensor_scalar(
                    norm2row_t[:, :], norm2row_t[:, :], 0.125, None, OP.mult),
                    reads=[norm2row_r], writes=[norm2row_r])
            for k in range(8):
                P.op("T", lambda E, k=k: E.matmul(
                    pbank[bankB][:, 0:TT4], selT[0:16, 128 * k:128 * (k + 1)],
                    norm2row_t[:, :], start=True, stop=True),
                    reads=[norm2row_r, selr], writes=[pbank_r[bankB]], label=f"nb{k}")
                P.op("V", lambda E, k=k: E.tensor_tensor(
                    post_t[:, k * TT4:(k + 1) * TT4], post_t[:, k * TT4:(k + 1) * TT4],
                    pbank[bankB][:, 0:TT4], OP.mult),
                    reads=[post_r[k], pbank_r[bankB]], writes=[post_r[k]], label=f"nm{k}")

        def emit_ab(tt, bank):
            for k in range(NCH):
                P.op("T", lambda E, k=k, bank=bank: E.matmul(
                    pbank[bank][0:48, 0:TT4], abw_t[:, k * 48:(k + 1) * 48],
                    xn_t[:, k * TT4:(k + 1) * TT4],
                    start=(k == 0), stop=(k == NCH - 1)),
                    reads=[abw_r, xn_r[k]], writes=[pbank_r[bank]], label=f"ab{k}", acc=(k != 0))
            sl = slice(tt * TT4, (tt + 1) * TT4)
            P.op("S", lambda E, sl=sl, bank=bank: E.activation(
                gb_t[32:48, sl], pbank[bank][32:48, 0:TT4], AF.Sigmoid),
                reads=[pbank_r[bank]], writes=[gb_r], label="beta")
            P.op("S", lambda E, bank=bank: E.activation(
                normrow_t[:, :], pbank[bank][0:16, 0:TT4], AF.Abs, bias=dtb_t[:, 0:1]),
                reads=[pbank_r[bank], dtb_r], writes=[normrow_r], label="spa")
            P.op("S", lambda E: E.activation(
                normrow_t[:, :], normrow_t[:, :], AF.Exp, scale=-1.0),
                reads=[normrow_r], writes=[normrow_r], label="spe")
            P.op("S", lambda E: E.activation(
                normrow_t[:, :], normrow_t[:, :], AF.Ln, bias=1.0),
                reads=[normrow_r, cbias_r], writes=[normrow_r], label="spl")
            P.op("S", lambda E, bank=bank: E.activation(
                norm2row_t[:, :], pbank[bank][0:16, 0:TT4], AF.Relu, bias=dtb_t[:, 0:1]),
                reads=[pbank_r[bank], dtb_r], writes=[norm2row_r], label="spr")
            P.op("V", lambda E: E.tensor_tensor(
                normrow_t[:, :], normrow_t[:, :], norm2row_t[:, :], OP.add),
                reads=[normrow_r, norm2row_r], writes=[normrow_r], label="spadd")
            P.op("V", lambda E: E.tensor_scalar(
                normrow_t[:, :], normrow_t[:, :], negea_t[:, 0:1], None, OP.mult),
                reads=[normrow_r, negea_r], writes=[normrow_r], label="gscl")
            for c in range(TT4 // CC):
                csl = slice(c * CC, (c + 1) * CC)
                P.op("V", lambda E, csl=csl, sl=sl: E.tensor_tensor_scan(
                    gb_t[0:16, sl][:, csl], normrow_t[:, csl], normrow_t[:, csl],
                    0.0, OP.add, OP.bypass),
                    reads=[normrow_r], writes=[gb_r], label=f"gc{c}")

        def emit_kbeta(tt, bank):
            selT = cn_t["c_selT"]; selr = cn_r["c_selT"]
            sl = slice(tt * TT4, (tt + 1) * TT4)
            P.op("V", lambda E, sl=sl: E.tensor_copy(normrow_t[:, :], gb_t[32:48, sl]),
                 reads=[gb_r], writes=[normrow_r], label="bcast_b")
            for k in range(8):
                P.op("T", lambda E, k=k, bank=bank: E.matmul(
                    pbank[bank][:, 0:TT4], selT[0:16, 128 * k:128 * (k + 1)],
                    normrow_t[:, :], start=True, stop=True),
                    reads=[normrow_r, selr], writes=[pbank_r[bank]], label=f"bb{k}")
                P.op("V", lambda E, k=k, bank=bank: E.tensor_tensor(
                    kbeta_t[:, k * TT4:(k + 1) * TT4], kpost_t[:, k * TT4:(k + 1) * TT4],
                    pbank[bank][:, 0:TT4], OP.mult),
                    reads=[kpost_r[k], pbank_r[bank]], writes=[kbeta_r[k]], label=f"kb{k}")

        for tt in range(NTT):
            emit_xn(tt, 4)
            emit_ab(tt, 5)
            proj_conv(tt, qw_t, qw_r, 8, 0, 0, qpost_t, qpost_r, 4)
            proj_conv(tt, kw_t, kw_r, 8, 32, 8, kpost_t, kpost_r, 5)
            proj_conv(tt, vw_t, vw_r, 16, 64, 16, vpost_t, vpost_r, 4)
            emit_norms(tt, qpost_t, qpost_r, True, 5, 4)
            emit_norms(tt, kpost_t, kpost_r, False, 5, 4)
            emit_kbeta(tt, 4)
            if tt == 0:
                tap("xn", xn_t[:, :], (128, 8 * TT4), xn_r, BF16)
                tap("q", qpost_t[:, :], (128, 8 * TT4), qpost_r, BF16)
                tap("k", kpost_t[:, :], (128, 8 * TT4), kpost_r, BF16)
                tap("v", vpost_t[:, :], (128, 16 * TT4), vpost_r, BF16)
            c0 = tt * 4
            for j in range(8):
                dma("Y", s_q[:, c0:c0 + 4, j, :], qpost_t[:, j * TT4:(j + 1) * TT4],
                    [sr_q[tt][j]], reads=[qpost_r[j]], label=f"sq{tt}_{j}")
                dma("Y", s_k[:, c0:c0 + 4, j, :], kpost_t[:, j * TT4:(j + 1) * TT4],
                    [sr_k[tt][j]], reads=[kpost_r[j]], label=f"sk{tt}_{j}")
                dma("Y", s_kb[:, c0:c0 + 4, j, :], kbeta_t[:, j * TT4:(j + 1) * TT4],
                    [sr_kb[tt][j]], reads=[kbeta_r[j]], label=f"skb{tt}_{j}")
            for j in range(16):
                dma("Y", s_v[:, c0:c0 + 4, j, :], vpost_t[:, j * TT4:(j + 1) * TT4],
                    [sr_v[tt][j]], reads=[vpost_r[j]], label=f"sv{tt}_{j}")
        tap("gb", gb_t[0:48, :], (48, T), [gb_r], F32)

    if "PS" in phases:
        AB.reset()
        q1_t, _r = AB.alloc("q1", 2 * 1024)
        q1_r = [Reg("q1a"), Reg("q1b")]
        for r in q1_r:
            r.rs = list(_r.rs)
        bindsplit(_r, q1_r)
        k1_t, _r = AB.alloc("k1", 2 * 1024)
        k1_r = [Reg("k1a"), Reg("k1b")]
        for r in k1_r:
            r.rs = list(_r.rs)
        bindsplit(_r, k1_r)
        kb1_t, _r = AB.alloc("kb1", 2 * 1024)
        kb1_r = [Reg("kb1a"), Reg("kb1b")]
        for r in kb1_r:
            r.rs = list(_r.rs)
        bindsplit(_r, kb1_r)
        v1_t, _r = AB.alloc("v1", 2 * 2048)
        v1_r = [Reg("v1a"), Reg("v1b")]
        for r in v1_r:
            r.rs = list(_r.rs)
        bindsplit(_r, v1_r)
        abar_t, _r = AB.alloc("abar", H * 128)
        abar_r = [Reg(f"ab{h}") for h in range(H)]
        for r in abar_r:
            r.rs = list(_r.rs)
        bindsplit(_r, abar_r)
        bbar_t, _r = AB.alloc("bbar", H * 128)
        bbar_r = [Reg(f"bb{h}") for h in range(H)]
        for r in bbar_r:
            r.rs = list(_r.rs)
        bindsplit(_r, bbar_r)
        z_t, _r = AB.alloc("z", H * 192)
        z_r = [Reg(f"z{h}") for h in range(H)]
        for r in z_r:
            r.rs = list(_r.rs)
        bindsplit(_r, z_r)
        la_t = []; la_r = []; ra_t = []; ra_r = []
        for i in range(2):
            t_, _r = AB.alloc(f"la{i}", H * 128)
            rr = [Reg(f"la{i}_{h}") for h in range(H)]
            for r in rr:
                r.rs = list(_r.rs)
            bindsplit(_r, rr)
            la_t.append(t_); la_r.append(rr)
            t_, _r = AB.alloc(f"ra{i}", H * 128)
            rr = [Reg(f"ra{i}_{h}") for h in range(H)]
            for r in rr:
                r.rs = list(_r.rs)
            bindsplit(_r, rr)
            ra_t.append(t_); ra_r.append(rr)
        ktok_t, _r = AB.alloc("ktok", H * DK)
        ktok_r = [Reg(f"kt{h}") for h in range(H)]
        for r in ktok_r:
            r.rs = list(_r.rs)
        bindsplit(_r, ktok_r)
        wT_t, _r = AB.alloc("wT", H * 128)
        wT_r = [Reg(f"wT{h}") for h in range(H)]
        for r in wT_r:
            r.rs = list(_r.rs)
        bindsplit(_r, wT_r)
        tt_t, _r = AB.alloc("tt", H * DV)
        tt_r = [Reg(f"tt{h}") for h in range(H)]
        for r in tt_r:
            r.rs = list(_r.rs)
        bindsplit(_r, tt_r)
        s_t, _r = AB.alloc("s", H * DV)
        s_r = [Reg(f"s{h}") for h in range(H)]
        for r in s_r:
            r.rs = list(_r.rs)
        bindsplit(_r, s_r)
        o_t, _or0 = AB.alloc("o_ws", H * DV)
        o_r = [Reg(f"o{h}") for h in range(H)]
        for r in o_r:
            r.rs = list(_or0.rs)
        bindsplit(_or0, o_r)
        AFA.reset()
        e_t, _r = AFA.alloc("e_ws", 4 * 129)
        e_r = [Reg(f"e{j}") for j in range(4)]
        for r in e_r:
            r.rs = list(_r.rs)
        bindsplit(_r, e_r)
        es_t, _r = AFA.alloc("es_ws", 4 * 128)
        es_r = [Reg(f"es{j}") for j in range(4)]
        for r in es_r:
            r.rs = list(_r.rs)
        bindsplit(_r, es_r)
        gbcolT_t, gbcolT_r = AFA.alloc("gbcolT", 48)
        gcolneg_t, gcolneg_r = AFA.alloc("gcolneg", 16)
        lamcolT_t, lamcolT_r = AFA.alloc("lamcolT", 16)
        lamC_t, lamC_r = AFA.alloc("lamC", 16)

        gw_t, gw_r = AB.alloc_at("gw", 8 * 2048, GW_OFS)
        opw_t, opw_r = AB.alloc_at("opw", 16 * 1024, GW_OFS + 16384)
        dma("G", gw_t[:, :], d_gw[:, :], [gw_r], label="ldgw")
        dma("G", opw_t[:, :], d_opw[:, :], [opw_r], label="ldopw")

        for h in range(H):
            P.op("V", lambda E, h=h: E.memset(s_t[:, h * DV:(h + 1) * DV], 0.0),
                 writes=[s_r[h]], label=f"s0_{h}")

        idb = cn_t["c_ident_bf"]; idbr = cn_r["c_ident_bf"]

        def emit_scan_chunk(c):
            bi = c % 2
            q1 = q1_t[:, bi * 1024:(bi + 1) * 1024]
            k1 = k1_t[:, bi * 1024:(bi + 1) * 1024]
            kb1 = kb1_t[:, bi * 1024:(bi + 1) * 1024]
            v1 = v1_t[:, bi * 1024:(bi + 1) * 1024 + (1024 if bi else -1024) + 1024] \
                if False else v1_t[:, bi * 2048:(bi + 1) * 2048]
            dma("Y", q1, s_q[:, c, :, :], [q1_r[bi]], reads=sr_q[c // 4],
                label=f"lq{c}")
            dma("Y", k1, s_k[:, c, :, :], [k1_r[bi]], reads=sr_k[c // 4],
                label=f"lk{c}")
            dma("Y", kb1, s_kb[:, c, :, :], [kb1_r[bi]], reads=sr_kb[c // 4],
                label=f"lkb{c}")
            dma("Y", v1, s_v[:, c, :, :], [v1_r[bi]], reads=sr_v[c // 4],
                label=f"lv{c}")
            csl = slice(c * CC, (c + 1) * CC)

            def hq(h):
                return (q1[(h % 2) * 64:(h % 2) * 64 + 64, (h // 2) * 128:(h // 2) * 128 + 128],
                        q1_r[bi])

            def hk(h):
                return (k1[(h % 2) * 64:(h % 2) * 64 + 64, (h // 2) * 128:(h // 2) * 128 + 128],
                        k1_r[bi])

            def hkb(h):
                return (kb1[(h % 2) * 64:(h % 2) * 64 + 64, (h // 2) * 128:(h // 2) * 128 + 128],
                        kb1_r[bi])

            def hv(h):
                return (v1[:, h * 128:(h + 1) * 128], v1_r[bi])

            P.op("T", lambda E, csl=csl: E.transpose(
                pbank[5][:, 0:48], gb_t[0:48, csl], cn_t["c_ident32"][0:48, 0:48]),
                reads=[gb_r, cn_r["c_ident32"]], writes=[pbank_r[5]], label=f"gT{c}")
            P.op("V", lambda E: E.tensor_copy(gbcolT_t[:, :], pbank[5][:, 0:48]),
                 reads=[pbank_r[5]], writes=[gbcolT_r], label=f"gTe{c}")
            P.op("V", lambda E: E.tensor_scalar(
                gcolneg_t[:, :], gbcolT_t[:, 0:16], -1.0, None, OP.mult),
                reads=[gbcolT_r], writes=[gcolneg_r])
            P.op("S", lambda E: E.activation(lamcolT_t[:, :], gbcolT_t[:, 0:16], AF.Exp),
                 reads=[gbcolT_r], writes=[lamcolT_r], label=f"lam{c}")
            if PS_STAGE < 1.1:
                return

            for hg in range(4):
                heads = range(4 * hg, 4 * hg + 4)
                b0, b1 = (0, 1) if hg % 2 == 0 else (2, 3)
                for ih, h in enumerate(heads):
                    bank = b0 if ih < 2 else b1
                    ofs = (ih % 2) * 258
                    if PS_STAGE < 1.2:
                        continue
                    P.op("T", lambda E, h=h, bank=bank, ofs=ofs: E.matmul(
                        pbank[bank][:, ofs:ofs + 129],
                        gbcolT_t[:, h:h + 1].broadcast_to([128, 128]),
                        cn_t["c_ident_ext"][:, 0:129], start=True, stop=False),
                        reads=[gbcolT_r, cn_r["c_ident_ext"]], writes=[pbank_r[bank]],
                        label=f"D1_{c}_{h}")
                    P.op("T", lambda E, bank=bank, ofs=ofs: E.matmul(
                        pbank[bank][:, ofs:ofs + 129],
                        cn_t["c_lowtri"][:, :], cn_t["c_upneg_ext"][:, 0:129],
                        start=False, stop=True),
                        reads=[cn_r["c_lowtri"], cn_r["c_upneg_ext"]],
                        writes=[pbank_r[bank]], label=f"D2_{c}_{h}", acc=True)
                    if PS_STAGE < 1.4:
                        continue
                    P.op("S", lambda E, ih=ih, h=h, bank=bank, ofs=ofs: E.activation(
                        e_t[:, ih * 129:(ih + 1) * 129], pbank[bank][:, ofs:ofs + 129],
                        AF.Exp, bias=gcolneg_t[:, h:h + 1]),
                        reads=[pbank_r[bank], gcolneg_r], writes=[e_r[ih]], label=f"E{c}_{h}")
                if PS_STAGE < 1.5:
                    continue
                for ih in range(4):
                    P.op("V", lambda E, ih=ih: E.tensor_tensor(
                        es_t[:, ih * 128:(ih + 1) * 128], e_t[:, ih * 129:ih * 129 + 128],
                        cn_t["c_ImOne"][:, :], OP.mult),
                        reads=[e_r[ih], cn_r["c_ImOne"]], writes=[es_r[ih]], label=f"Es{c}_{hg}_{ih}")
                if PS_STAGE < 1.6:
                    continue
                for ih, h in enumerate(heads):
                    kbap, kbr = hkb(h); kap, kr = hk(h); qap, qr = hq(h)
                    kkb, kko = (4, ih * 128)
                    if KKVAR == 1:
                        kkb, kko = (4, 0)
                    elif KKVAR == 2:
                        kkb, kko = (4 + ih % 2, (ih // 2) * 128)
                    kkrd = [kbr, kr] + ([pbank_r[kkb]] if KKVAR == 3 else [])
                    P.op("T", lambda E, kbap=kbap, kap=kap, kkb=kkb, kko=kko: E.matmul(
                        pbank[kkb][:, kko:kko + 128], kbap, kap,
                        start=True, stop=True),
                        reads=kkrd, writes=[pbank_r[kkb]], label=f"KK{c}_{h}")
                    if PS_STAGE < 1.64:
                        continue
                    P.op("T", lambda E, kap=kap, qap=qap, ih=ih: E.matmul(
                        pbank[5][:, ih * 128:(ih + 1) * 128], kap, qap,
                        start=True, stop=True),
                        reads=[kr, qr], writes=[pbank_r[5]], label=f"QK{c}_{h}")
                if PS_STAGE < 1.66:
                    continue
                for ih, h in enumerate(heads):
                    P.op("V", lambda E, ih=ih, h=h: E.tensor_tensor(
                        abar_t[:, h * 128:(h + 1) * 128], es_t[:, ih * 128:(ih + 1) * 128],
                        pbank[4][:, ih * 128:(ih + 1) * 128], OP.mult),
                        reads=[es_r[ih], pbank_r[4]], writes=[abar_r[h]], label=f"Ab{c}_{h}")
                    P.op("V", lambda E, ih=ih, h=h: E.tensor_tensor(
                        bbar_t[:, h * 128:(h + 1) * 128], e_t[:, ih * 129:ih * 129 + 128],
                        pbank[5][:, ih * 128:(ih + 1) * 128], OP.mult),
                        reads=[e_r[ih], pbank_r[5]], writes=[bbar_r[h]], label=f"Bb{c}_{h}")
                if PS_STAGE < 1.68:
                    continue
                for ih, h in enumerate(heads):
                    P.op("V", lambda E, ih=ih, h=h: E.tensor_scalar(
                        lamC_t[:, h:h + 1], e_t[:, ih * 129 + 128: ih * 129 + 129],
                        lamcolT_t[:, h:h + 1], None, OP.mult),
                        reads=[e_r[ih], lamcolT_r], writes=[lamC_r], label=f"lC{c}_{h}")
                if PS_STAGE < 1.8:
                    continue
                for ih, h in enumerate(heads):
                    vap, vr = hv(h)
                    P.op("T", lambda E, vap=vap, ih=ih: E.transpose(
                        pbf[0][:, ih * 256:ih * 256 + 128], vap, idb[:, :]),
                        reads=[vr, idbr], writes=[pbf_r[0]], label=f"vT{c}_{h}")
                    kap, kr = hk(h)
                    kb_ = (h % 2) * 64
                    P.op("T", lambda E, kap=kap, ih=ih, kb_=kb_: E.transpose(
                        pbf[0][:, ih * 256 + 128:ih * 256 + 192], kap,
                        cn_t["c_ident2"][kb_:kb_ + 64, 0:64]),
                        reads=[kr, idbr, cn_r["c_ident2"]], writes=[pbf_r[0]],
                        label=f"kT{c}_{h}")
                for ih, h in enumerate(heads):
                    P.op("V", lambda E, ih=ih, h=h: E.tensor_copy(
                        z_t[:, h * 192:h * 192 + 128], pbf[0][:, ih * 256:ih * 256 + 128]),
                        reads=[pbf_r[0]], writes=[z_r[h]], label=f"zv{c}_{h}")
                    P.op("V", lambda E, ih=ih, h=h: E.tensor_scalar(
                        z_t[:, h * 192 + 128:h * 192 + 192],
                        pbf[0][:, ih * 256 + 128:ih * 256 + 192],
                        lamcolT_t[:, h:h + 1], None, OP.mult),
                        reads=[pbf_r[0], lamcolT_r], writes=[z_r[h]], label=f"zk{c}_{h}")
                    P.op("V", lambda E, ih=ih, h=h: E.tensor_scalar(
                        ktok_t[:, h * DK:(h + 1) * DK],
                        pbf[0][:, ih * 256 + 128:ih * 256 + 192],
                        e_t[:, ih * 129 + 127: ih * 129 + 128], None, OP.mult),
                        reads=[pbf_r[0], e_r[ih]], writes=[ktok_r[h]], label=f"kt{c}_{h}")
                if PS_STAGE < 1.9:
                    continue
                for ih, h in enumerate(heads):
                    P.op("T", lambda E, h=h, ih=ih: E.transpose(
                        pbf[1][:, ih * 128:(ih + 1) * 128],
                        abar_t[:, h * 128:(h + 1) * 128], idb[:, :]),
                        reads=[abar_r[h], idbr], writes=[pbf_r[1]], label=f"r0T{c}_{h}")
                P.op("V", lambda E, heads=heads: E.tensor_copy(
                    ra_t[0][:, heads[0] * 128:(heads[0] + 4) * 128], pbf[1][:, 0:512]),
                    reads=[pbf_r[1]], writes=[ra_r[0][h] for h in heads], label=f"r0e{c}_{hg}")

            if PS_STAGE < 3:
                return

            def apply_level(lev, src_tiles, src_regs):
                for hg in range(4):
                    heads = range(4 * hg, 4 * hg + 4)
                    bpair = (0, 1) if hg % 2 == 0 else (2, 3)
                    for ih, h in enumerate(heads):
                        bb = bpair[0] if ih < 2 else bpair[1]
                        ofs = (ih % 2) * 192
                        P.op("T", lambda E, h=h, bb=bb, ofs=ofs: E.matmul(
                            pbank[bb][:, ofs:ofs + 192],
                            src_tiles[:, h * 128:(h + 1) * 128],
                            z_t[:, h * 192:(h + 1) * 192], start=True, stop=True),
                            reads=[src_regs[h], z_r[h]], writes=[pbank_r[bb]],
                            label=f"ap{lev}_{c}_{h}")
                    for ih, h in enumerate(heads):
                        bb = bpair[0] if ih < 2 else bpair[1]
                        ofs = (ih % 2) * 192
                        P.op("V", lambda E, h=h, bb=bb, ofs=ofs: E.tensor_tensor(
                            z_t[:, h * 192:(h + 1) * 192], z_t[:, h * 192:(h + 1) * 192],
                            pbank[bb][:, ofs:ofs + 192], OP.add),
                            reads=[pbank_r[bb], z_r[h]], writes=[z_r[h]],
                            label=f"ape{lev}_{c}_{h}")

            def square_level(lev, srcL, srcLr, srcR, srcRr, dstL, dstLr, dstR, dstRr, last):
                for hg in range(4):
                    heads = range(4 * hg, 4 * hg + 4)
                    for ih, h in enumerate(heads):
                        P.op("T", lambda E, h=h, ih=ih: E.matmul(
                            pbank[4][:, ih * 128:(ih + 1) * 128],
                            srcR[:, h * 128:(h + 1) * 128], srcL[:, h * 128:(h + 1) * 128],
                            start=True, stop=True),
                            reads=[srcRr[h], srcLr[h]], writes=[pbank_r[4]], label=f"qL{lev}_{c}_{h}")
                        if not last:
                            P.op("T", lambda E, h=h, ih=ih: E.matmul(
                                pbank[5][:, ih * 128:(ih + 1) * 128],
                                srcL[:, h * 128:(h + 1) * 128], srcR[:, h * 128:(h + 1) * 128],
                                start=True, stop=True),
                                reads=[srcRr[h], srcLr[h]], writes=[pbank_r[5]], label=f"qR{lev}_{c}_{h}")
                    h0 = heads[0]
                    P.op("V", lambda E, h0=h0: E.tensor_copy(
                        dstL[:, h0 * 128:(h0 + 4) * 128], pbank[4][:, 0:512]),
                        reads=[pbank_r[4]], writes=[dstLr[h] for h in heads], label=f"qLe{lev}_{c}_{hg}")
                    if not last:
                        P.op("S", lambda E, h0=h0: E.activation(
                            dstR[:, h0 * 128:(h0 + 4) * 128], pbank[5][:, 0:512], AF.Copy),
                            reads=[pbank_r[5]], writes=[dstRr[h] for h in heads], label=f"qRe{lev}_{c}_{hg}")

            apply_level(0, abar_t, abar_r)
            curL, curLr = abar_t, abar_r
            curR, curRr = ra_t[0], ra_r[0]
            for lev in range(1, 7):
                par = lev % 2
                last = (lev == 6)
                square_level(lev, curL, curLr, curR, curRr,
                             la_t[par], la_r[par], ra_t[par], ra_r[par], last)
                apply_level(lev, la_t[par], la_r[par])
                curL, curLr = la_t[par], la_r[par]
                curR, curRr = ra_t[par], ra_r[par]
            if PS_STAGE < 4:
                return
            for hg in range(4):
                heads = range(4 * hg, 4 * hg + 4)
                bpair = (0, 1) if hg % 2 == 0 else (2, 3)
                for ih, h in enumerate(heads):
                    P.op("T", lambda E, h=h, ih=ih: E.transpose(
                        pbf[1][0:64, ih * 128:(ih + 1) * 128],
                        z_t[:, h * 192 + 128:h * 192 + 192], idb[:, :]),
                        reads=[z_r[h], idbr], writes=[pbf_r[1]], label=f"wT{c}_{h}")
                P.op("V", lambda E, heads=heads: E.tensor_copy(
                    wT_t[0:64, heads[0] * 128:(heads[0] + 4) * 128], pbf[1][0:64, 0:512]),
                    reads=[pbf_r[1]], writes=[wT_r[h] for h in heads], label=f"wTe{c}_{hg}")
                for ih, h in enumerate(heads):
                    bb = bpair[0] if ih < 2 else bpair[1]
                    ofs = (ih % 2) * 128
                    P.op("T", lambda E, h=h, bb=bb, ofs=ofs: E.matmul(
                        pbank[bb][:, ofs:ofs + 128], wT_t[0:64, h * 128:(h + 1) * 128],
                        s_t[0:64, h * DV:(h + 1) * DV], start=True, stop=True),
                        reads=[wT_r[h], s_r[h]], writes=[pbank_r[bb]], label=f"ws{c}_{h}")
                for ih, h in enumerate(heads):
                    bb = bpair[0] if ih < 2 else bpair[1]
                    ofs = (ih % 2) * 128
                    P.op("V", lambda E, h=h, bb=bb, ofs=ofs: E.tensor_tensor(
                        tt_t[:, h * DV:(h + 1) * DV], z_t[:, h * 192:h * 192 + 128],
                        pbank[bb][:, ofs:ofs + 128], OP.subtract),
                        reads=[z_r[h], pbank_r[bb]], writes=[tt_r[h]], label=f"ts{c}_{h}")
                    P.op("V", lambda E, h=h: E.tensor_scalar(
                        tt_t[:, h * DV:(h + 1) * DV], tt_t[:, h * DV:(h + 1) * DV],
                        gbcolT_t[:, 32 + h:33 + h], None, OP.mult),
                        reads=[tt_r[h], gbcolT_r], writes=[tt_r[h]], label=f"tb{c}_{h}")
            for hg in range(4):
                heads = range(4 * hg, 4 * hg + 4)
                bpair = (0, 1) if hg % 2 == 0 else (2, 3)
                for ih, h in enumerate(heads):
                    bb = bpair[0] if ih < 2 else bpair[1]
                    ofs = (ih % 2) * 128
                    qap, qr = hq(h)
                    qb_ = (h % 2) * 64
                    P.op("T", lambda E, qap=qap, bb=bb, ofs=ofs, h=h, qb_=qb_: E.matmul(
                        pbank[bb][:, ofs:ofs + 128], qap,
                        s_t[qb_:qb_ + 64, h * DV:(h + 1) * DV],
                        start=True, stop=True),
                        reads=[qr, s_r[h]], writes=[pbank_r[bb]], label=f"qs{c}_{h}")
                    bb2 = 4 if ih < 2 else 5
                    ofs2 = (ih % 2) * 128
                    P.op("T", lambda E, h=h, bb2=bb2, ofs2=ofs2: E.matmul(
                        pbank[bb2][:, ofs2:ofs2 + 128], bbar_t[:, h * 128:(h + 1) * 128],
                        tt_t[:, h * DV:(h + 1) * DV], start=True, stop=True),
                        reads=[bbar_r[h], tt_r[h]], writes=[pbank_r[bb2]], label=f"bt{c}_{h}")
                for ih, h in enumerate(heads):
                    bb = bpair[0] if ih < 2 else bpair[1]
                    ofs = (ih % 2) * 128
                    bb2 = 4 if ih < 2 else 5
                    ofs2 = (ih % 2) * 128
                    P.op("S", lambda E, h=h, bb=bb, ofs=ofs: E.activation(
                        o_t[:, h * DV:(h + 1) * DV], pbank[bb][:, ofs:ofs + 128],
                        AF.Identity, scale=lamcolT_t[:, h:h + 1]),
                        reads=[pbank_r[bb], lamcolT_r], writes=[o_r[h]],
                        label=f"oe{c}_{h}")
                    P.op("V", lambda E, h=h, bb2=bb2, ofs2=ofs2: E.tensor_tensor(
                        o_t[:, h * DV:(h + 1) * DV], o_t[:, h * DV:(h + 1) * DV],
                        pbank[bb2][:, ofs2:ofs2 + 128], OP.add),
                        reads=[o_r[h], pbank_r[bb2]], writes=[o_r[h]],
                        label=f"oa{c}_{h}")
                for ih, h in enumerate(heads):
                    bb = bpair[0] if ih < 2 else bpair[1]
                    ofs = (ih % 2) * 128
                    P.op("T", lambda E, h=h, bb=bb, ofs=ofs: E.matmul(
                        pbank[bb][0:64, ofs:ofs + 128], ktok_t[:, h * DK:(h + 1) * DK],
                        tt_t[:, h * DV:(h + 1) * DV], start=True, stop=True),
                        reads=[ktok_r[h], tt_r[h]], writes=[pbank_r[bb]], label=f"ku{c}_{h}")
                    P.op("T", lambda E, h=h, bb=bb, ofs=ofs: E.matmul(
                        pbank[bb][64:128, ofs + 256:ofs + 384],
                        ktok_t[:, h * DK:(h + 1) * DK],
                        tt_t[:, h * DV:(h + 1) * DV], start=True, stop=True,
                        tile_position=(0, 64)),
                        reads=[ktok_r[h], tt_r[h]], writes=[pbank_r[bb]], label=f"ku2{c}_{h}")
                for ih, h in enumerate(heads):
                    bb = bpair[0] if ih < 2 else bpair[1]
                    ofs = (ih % 2) * 128
                    P.op("V", lambda E, h=h, bb=bb, ofs=ofs: E.scalar_tensor_tensor(
                        s_t[0:64, h * DV:(h + 1) * DV], s_t[0:64, h * DV:(h + 1) * DV],
                        lamC_t[0:64, h:h + 1], pbank[bb][0:64, ofs:ofs + 128],
                        OP.mult, OP.add),
                        reads=[s_r[h], lamC_r, pbank_r[bb]], writes=[s_r[h]],
                        label=f"su{c}_{h}")
                    P.op("V", lambda E, h=h, bb=bb, ofs=ofs: E.scalar_tensor_tensor(
                        s_t[64:128, h * DV:(h + 1) * DV], s_t[64:128, h * DV:(h + 1) * DV],
                        lamC_t[64:128, h:h + 1], pbank[bb][64:128, ofs + 256:ofs + 384],
                        OP.mult, OP.add),
                        reads=[s_r[h], lamC_r, pbank_r[bb]], writes=[s_r[h]],
                        label=f"su2{c}_{h}")
            dma("Y", s_o[:, c, :, :], o_t[:, :], [sr_o[c]], reads=o_r,
                label=f"so{c}")

        for c in range(NT):
            emit_scan_chunk(c)
        tap("o_ws", o_t[:, :], (128, H * DV), o_r, BF16)

    if "PA2" in phases:
        AB.reset()
        if "PS" not in phases:
            gw_t, gw_r = AB.alloc_at("gw", 8 * 2048, GW_OFS)
            opw_t, opw_r = AB.alloc_at("opw", 16 * 1024, GW_OFS + 16384)
            dma("G", gw_t[:, :], d_gw[:, :], [gw_r], label="ldgw")
            dma("G", opw_t[:, :], d_opw[:, :], [opw_r], label="ldopw")
        outw_t, outw_r = AB.alloc("outw", 8 * 1024)
        dma("G", outw_t[:, :], d_outw[:, :], [outw_r], label="ldoutw")
        xn2_t, xn2_r = AB.alloc("xn2", 8 * TT4)
        gate_t, _r = AB.alloc("gate", 16 * TT4)
        gate_r = [Reg(f"g{j}") for j in range(16)]
        for r in gate_r:
            r.rs = list(_r.rs)
        bindsplit(_r, gate_r)
        otok_t, otok_r = AB.alloc("otok", 16 * 128)
        o2_t, o2_r = AB.alloc("o2", 16 * 128)
        ofm_t, _r = AB.alloc("ofm", 16 * TT4)
        ofm_r = [Reg(f"of{j}") for j in range(16)]
        for r in ofm_r:
            r.rs = list(_r.rs)
        bindsplit(_r, ofm_r)
        y0_t, y0_r = AB.alloc("y0", 8 * TT4)
        x2b_t, x2b_r = AB.alloc("x2b", 8 * TT4)
        AFA.reset()
        osq_t, osq_r = AFA.alloc("osq", 16 * 128)
        xr_t, xr_r = AFA.alloc("xr", 8 * TT4)
        sq2_t, sq2_r = AFA.alloc("sq2", TT4)
        ons_t, ons_r = AFA.alloc("ons", 16)

        for k in range(16):
            P.op("V", lambda E, k=k: E.tensor_scalar(
                opw_t[:, k * 1024:(k + 1) * 1024], opw_t[:, k * 1024:(k + 1) * 1024],
                onw_t[:, 0:1], None, OP.mult),
                reads=[opw_r, onw_r], writes=[opw_r], label=f"opsc{k}")

        for tt in range(NTT):
            ts = tt * TT4
            dma("Y", xn2_t[:, :], s_xn[:, :, ts:ts + TT4], [xn2_r],
                reads=[sr_xn[tt]], label=f"lxn{tt}")
            for j in range(16):
                bank = j % 2
                for k in range(NCH):
                    P.op("T", lambda E, j=j, k=k, bank=bank: E.matmul(
                        pbank[bank][:, 0:TT4],
                        gw_t[:, k * 2048 + j * 128: k * 2048 + j * 128 + 128],
                        xn2_t[:, k * TT4:(k + 1) * TT4],
                        start=(k == 0), stop=(k == NCH - 1)),
                        reads=[gw_r, xn2_r], writes=[pbank_r[bank]], label=f"gm{j}_{k}", acc=(k != 0))
                P.op("S", lambda E, j=j, bank=bank: E.activation(
                    gate_t[:, j * TT4:(j + 1) * TT4], pbank[bank][:, 0:TT4], AF.Silu),
                    reads=[pbank_r[bank]], writes=[gate_r[j]], label=f"gs{j}")
            for ci in range(4):
                c = tt * 4 + ci
                dma("Y", otok_t[:, :], s_o[:, c, :, :], [otok_r],
                    reads=[sr_o[c]], label=f"lo{c}")
                P.op("S", lambda E: E.activation(osq_t[:, :], otok_t[:, :], AF.Square),
                     reads=[otok_r], writes=[osq_r], label=f"oq{c}")
                for h in range(16):
                    P.op("V", lambda E, h=h: E.reduce_sum(
                        ons_t[:, h:h + 1], osq_t[:, h * 128:(h + 1) * 128], AX.X),
                        reads=[osq_r], writes=[ons_r], label=f"or{c}_{h}")
                P.op("S", lambda E: E.activation(ons_t[:, :], ons_t[:, :], AF.Sqrt,
                                                 bias=1e-5, scale=1.0 / 128.0),
                     reads=[ons_r, cbias_r], writes=[ons_r], label=f"onsq{c}")
                P.op("V", lambda E: E.reciprocal(ons_t[:, :], ons_t[:, :]),
                     reads=[ons_r], writes=[ons_r], label=f"ons{c}")
                for h in range(16):
                    pb_i = h % 2
                    P.op("T", lambda E, h=h, ci=ci, pb_i=pb_i: E.transpose(
                        pbf[pb_i][:, (h % 4) // 2 * 128:(h % 4) // 2 * 128 + 128]
                        if False else pbf[pb_i][:, ((h // 2) % 4) * 128:((h // 2) % 4) * 128 + 128],
                        gate_t[:, h * TT4 + ci * 128: h * TT4 + ci * 128 + 128], idb[:, :]),
                        reads=[gate_r[h], idbr], writes=[pbf_r[pb_i]], label=f"gT{c}_{h}")
                    P.op("V", lambda E, h=h, pb_i=pb_i: E.scalar_tensor_tensor(
                        o2_t[:, h * 128:(h + 1) * 128], otok_t[:, h * 128:(h + 1) * 128],
                        ons_t[:, h:h + 1],
                        pbf[pb_i][:, ((h // 2) % 4) * 128:((h // 2) % 4) * 128 + 128],
                        OP.mult, OP.mult),
                        reads=[otok_r, ons_r, pbf_r[pb_i]], writes=[o2_r], label=f"o2{c}_{h}")
                for h in range(16):
                    pb_i = h % 2
                    P.op("T", lambda E, h=h, pb_i=pb_i: E.transpose(
                        pbf[pb_i][:, (((h // 2) % 4) + 4) * 128:(((h // 2) % 4) + 4) * 128 + 128],
                        o2_t[:, h * 128:(h + 1) * 128], idb[:, :]),
                        reads=[o2_r, idbr], writes=[pbf_r[pb_i]], label=f"oT{c}_{h}")
                    P.op("V", lambda E, h=h, ci=ci, pb_i=pb_i: E.tensor_copy(
                        ofm_t[:, h * TT4 + ci * 128: h * TT4 + ci * 128 + 128],
                        pbf[pb_i][:, (((h // 2) % 4) + 4) * 128:(((h // 2) % 4) + 4) * 128 + 128]),
                        reads=[pbf_r[pb_i]], writes=[ofm_r[h]], label=f"oTe{c}_{h}")
            if tt == 0:
                tap("ofm", ofm_t[:, :], (128, 16 * TT4), ofm_r, BF16)
            for j in range(8):
                bank = j % 2
                for k in range(16):
                    P.op("T", lambda E, j=j, k=k, bank=bank: E.matmul(
                        pbank[bank][:, 0:TT4],
                        opw_t[:, k * 1024 + j * 128: k * 1024 + j * 128 + 128],
                        ofm_t[:, k * TT4:(k + 1) * TT4],
                        start=(k == 0), stop=(k == 15)),
                        reads=[opw_r, ofm_r[k]], writes=[pbank_r[bank]], label=f"op{j}_{k}", acc=(k != 0))
                P.op("S", lambda E, j=j, bank=bank: E.activation(
                    y0_t[:, j * TT4:(j + 1) * TT4], pbank[bank][:, 0:TT4], AF.Copy),
                    reads=[pbank_r[bank]], writes=[y0_r], label=f"y0{j}")
            dma("Y", xr_t[:, :], xin[:, :, ts:ts + TT4], [xr_r], label=f"lx2{tt}")
            for j in range(8):
                bank = 2 + (j % 2)
                for k in range(NCH):
                    P.op("T", lambda E, j=j, k=k, bank=bank: E.matmul(
                        pbank[bank][:, 0:TT4],
                        outw_t[:, k * 1024 + j * 128: k * 1024 + j * 128 + 128],
                        y0_t[:, k * TT4:(k + 1) * TT4],
                        start=(k == 0), stop=(k == NCH - 1)),
                        reads=[outw_r, y0_r], writes=[pbank_r[bank]], label=f"ou{j}_{k}", acc=(k != 0))
                P.op("V", lambda E, j=j, bank=bank: E.tensor_tensor(
                    xr_t[:, j * TT4:(j + 1) * TT4], xr_t[:, j * TT4:(j + 1) * TT4],
                    pbank[bank][:, 0:TT4], OP.add),
                    reads=[xr_r, pbank_r[bank]], writes=[xr_r], label=f"res{j}")
            if tt == 0:
                tap("attn0", xr_t[:, :], (128, 8 * TT4), [xr_r], F32)
            emit_ln(TT4, xr_t, xr_r, x2b_t, x2b_r, 4, 5, sq2_t, sq2_r)
            dma("Y", s_x2[:, :, ts:ts + TT4], x2b_t[:, :], [sr_x2[tt]],
                reads=[x2b_r],
                label=f"sx2{tt}")
            if tt == 0:
                tap("x2", x2b_t[:, :], (128, 8 * TT4), [x2b_r], BF16)

    if "PB" in phases:
        AB.reset()
        hpm_t, hpm_r = AB.alloc("hpm", 8 * 1024)
        tern_t, tern_r = AB.alloc("tern0", 8 * 1024)
        x2t_t, x2t_r = AB.alloc("x2t", 8 * TT4)
        xqf_t, xqf_r = AB.alloc("xqf", 8 * TT4)
        xqtk_t, xqtk_r = AB.alloc("xqtk", 1024)
        xrb_t, xrb_r = AB.alloc("xrb", 8 * TT4)
        dma("Y", hpm_t[:, :], d_hpm[:, :], [hpm_r], label="ldhpm")
        AFA.reset()
        wf_t, wf_r = AFA.alloc("wf", 2048)
        wabs_t, wabs_r = AFA.alloc("wabs", 1024)
        xh_t, xh_r = AFA.alloc("xh", 1024)
        mrow_t, mrow_r = AFA.alloc("mrow", 128)
        scbc_t, scbc_r = AFA.alloc("scbc", TT4)
        rsc_t, rsc_r = AFA.alloc("rsc", 1024)
        for k in range(NCH):
            pi = k % 2
            dma("Y", wf_t[:, pi * 1024:(pi + 1) * 1024],
                d_ffw[:, k * 1024:(k + 1) * 1024], [wf_r], label=f"lfw{k}")
            P.op("S", lambda E, pi=pi: E.activation(
                wabs_t[:, :], wf_t[:, pi * 1024:(pi + 1) * 1024], AF.Abs),
                reads=[wf_r], writes=[wabs_r], label=f"fab{k}")
            for half in range(2):
                P.op("T", lambda E, k=k, half=half: E.matmul(
                    pbank[half][0:1, 0:512], cn_t["c_ones128"][:, 0:1],
                    wabs_t[:, half * 512:(half + 1) * 512],
                    start=(k == 0), stop=(k == NCH - 1)),
                    reads=[wabs_r, cn_r["c_ones128"]], writes=[pbank_r[half]],
                    label=f"fsum{k}_{half}", acc=(k != 0))
        for half in range(2):
            P.op("V", lambda E, half=half: E.tensor_scalar(
                rowC[:, half * 512:(half + 1) * 512], pbank[half][0:1, 0:512],
                1.0 / 1024.0, 1e-5, OP.mult, OP.max),
                reads=[pbank_r[half]], writes=[rowC_r], label=f"fsc{half}")
        P.op("V", lambda E: E.reciprocal(rowD[:, 0:1024] if TT4 >= 1024 else rowD[:, :],
                                         rowC[:, 0:1024] if TT4 >= 1024 else rowC[:, :]),
             reads=[rowC_r], writes=[rowD_r], label="frsc")
        for half in range(2):
            P.op("T", lambda E, half=half: E.matmul(
                pbank[2][:, 0:512], cn_t["c_ones1"][0:1, :],
                rowD[:, half * 512:(half + 1) * 512], start=True, stop=True),
                reads=[rowD_r, cn_r["c_ones1"]], writes=[pbank_r[2]], label=f"fbc{half}")
            P.op("V", lambda E, half=half: E.tensor_copy(
                rsc_t[:, half * 512:(half + 1) * 512], pbank[2][:, 0:512]),
                reads=[pbank_r[2]], writes=[rsc_r], label=f"fbce{half}")
        for k in range(8):
            P.op("T", lambda E, k=k: E.matmul(
                pbank[3][0:1, k * 64:k * 64 + 1] if False else pbank[3][:, k:k + 1],
                rowC[:, k * 128:(k + 1) * 128].broadcast_to([1, 128])
                if False else rowC[:, k * 128:(k + 1) * 128],
                cn_t["c_ones128"][0:1, 0:1].broadcast_to([1, 1])
                if False else cn_t["c_ones1"][0:1, 0:1],
                start=True, stop=True),
                reads=[rowC_r, cn_r["c_ones1"]], writes=[pbank_r[3]], label=f"fcol{k}")
        P.op("V", lambda E: E.tensor_copy(col_t[:, 0:8], pbank[3][:, 0:8]),
             reads=[pbank_r[3]], writes=[col_r[0]], label="fcole")
        for k in range(NCH):
            pi = k % 2
            dma("Y", wf_t[:, pi * 1024:(pi + 1) * 1024],
                d_ffw[:, k * 1024:(k + 1) * 1024], [wf_r], label=f"lfw2_{k}")
            P.op("V", lambda E, pi=pi: E.scalar_tensor_tensor(
                wabs_t[:, :], wf_t[:, pi * 1024:(pi + 1) * 1024], 1.0, rsc_t[:, :],
                OP.mult, OP.mult),
                reads=[wf_r, rsc_r], writes=[wabs_r], label=f"tn0_{k}")
            P.op("V", lambda E, pi=pi: E.tensor_scalar(
                wf_t[:, pi * 1024:(pi + 1) * 1024], wabs_t[:, :], M52, -M52,
                OP.add, OP.add),
                reads=[wabs_r], writes=[wf_r], label=f"tn2_{k}")
            P.op("V", lambda E: E.tensor_scalar(
                wabs_t[:, :], wabs_t[:, :], M23, -M23, OP.add, OP.add),
                reads=[wabs_r], writes=[wabs_r], label=f"tn3_{k}")
            P.op("V", lambda E, pi=pi: E.scalar_tensor_tensor(
                wabs_t[:, :], wabs_t[:, :], selc_t[:, 1:2],
                wf_t[:, pi * 1024:(pi + 1) * 1024], OP.mult, OP.add),
                reads=[wabs_r, selc_r, wf_r], writes=[wabs_r], label=f"tn3b_{k}")
            P.op("V", lambda E, k=k: E.tensor_scalar(
                tern_t[:, k * 1024:(k + 1) * 1024], wabs_t[:, :], -1.0, 1.0,
                OP.max, OP.min),
                reads=[wabs_r], writes=[tern_r], label=f"tn4_{k}")
        tap("tern0", tern_t[:, :], (128, 8 * 1024), [tern_r], BF16)
        tap("fcol", col_t[:, 0:8], (128, 8), [col_r[0]], F32)

        for tt in range(NTT):
            ts = tt * TT4
            dma("Y", x2t_t[:, :], s_x2[:, :, ts:ts + TT4], [x2t_r],
                reads=[sr_x2[tt]], label=f"lx2t{tt}")
            for ci in range(TT4 // 128):
                tc = ts + ci * 128
                for half in range(2):
                    for k in range(NCH):
                        P.op("T", lambda E, k=k, ci=ci, half=half: E.matmul(
                            pbank[half][:, 0:512],
                            x2t_t[:, k * TT4 + ci * 128: k * TT4 + ci * 128 + 128],
                            hpm_t[:, k * 1024 + half * 512: k * 1024 + (half + 1) * 512],
                            start=(k == 0), stop=(k == NCH - 1)),
                            reads=[x2t_r, hpm_r], writes=[pbank_r[half]],
                            label=f"hd{tt}_{ci}_{half}_{k}", acc=(k != 0))
                for half in range(2):
                    P.op("V", lambda E, half=half: E.tensor_scalar(
                        xh_t[:, half * 512:(half + 1) * 512], pbank[half][:, 0:512],
                        1.0 / 32.0, None, OP.mult),
                        reads=[pbank_r[half]], writes=[xh_r], label=f"he{tt}_{ci}_{half}")
                P.op("V", lambda E: E.tensor_reduce(
                    mrow_t[:, 0:1], xh_t[:, :], AX.X, OP.max, apply_absolute_value=True),
                    reads=[xh_r], writes=[mrow_r], label=f"hm{tt}_{ci}")
                P.op("V", lambda E: E.tensor_scalar(
                    mrow_t[:, 1:2], mrow_t[:, 0:1], 1e-5, 1.0 / 7.0, OP.max, OP.mult),
                    reads=[mrow_r], writes=[mrow_r], label=f"hs{tt}_{ci}")
                P.op("V", lambda E: E.reciprocal(mrow_t[:, 2:3], mrow_t[:, 1:2]),
                     reads=[mrow_r], writes=[mrow_r], label=f"hr{tt}_{ci}")
                P.op("V", lambda E: E.tensor_scalar(
                    xh_t[:, :], xh_t[:, :], mrow_t[:, 2:3], None, OP.mult),
                    reads=[xh_r, mrow_r], writes=[xh_r], label=f"hv{tt}_{ci}")
                P.op("V", lambda E: E.tensor_scalar(
                    wabs_t[:, :], xh_t[:, :], M52, -M52, OP.add, OP.add),
                    reads=[xh_r], writes=[wabs_r], label=f"hmd{tt}_{ci}")
                P.op("V", lambda E: E.tensor_scalar(
                    xh_t[:, :], xh_t[:, :], M23, -M23, OP.add, OP.add),
                    reads=[xh_r], writes=[xh_r], label=f"hq{tt}_{ci}")
                P.op("V", lambda E: E.scalar_tensor_tensor(
                    xh_t[:, :], xh_t[:, :], selc_t[:, 1:2], wabs_t[:, :],
                    OP.mult, OP.add),
                    reads=[xh_r, selc_r, wabs_r], writes=[xh_r], label=f"hq2{tt}_{ci}")
                P.op("V", lambda E: E.tensor_scalar(
                    xqtk_t[:, :], xh_t[:, :], -8.0, 7.0, OP.max, OP.min),
                    reads=[xh_r], writes=[xqtk_r], label=f"hc{tt}_{ci}")
                P.op("T", lambda E, ci=ci: E.matmul(
                    pbank[2][0:1, ci * 128:(ci + 1) * 128], mrow_t[:, 1:2],
                    cn_t["c_ident32"][:, :], start=True, stop=True),
                    reads=[mrow_r, cn_r["c_ident32"]], writes=[pbank_r[2]],
                    label=f"hsr{tt}_{ci}")
                for k in range(8):
                    pb_i = k % 2
                    P.op("T", lambda E, k=k, pb_i=pb_i: E.transpose(
                        pbf[pb_i][:, (k // 2) * 128:(k // 2) * 128 + 128],
                        xqtk_t[:, k * 128:(k + 1) * 128], idb[:, :]),
                        reads=[xqtk_r, idbr], writes=[pbf_r[pb_i]], label=f"xqT{tt}_{ci}_{k}")
                    P.op("V", lambda E, k=k, ci=ci, pb_i=pb_i: E.tensor_copy(
                        xqf_t[:, k * TT4 + ci * 128: k * TT4 + ci * 128 + 128],
                        pbf[pb_i][:, (k // 2) * 128:(k // 2) * 128 + 128]),
                        reads=[pbf_r[pb_i]], writes=[xqf_r], label=f"xqe{tt}_{ci}_{k}")
            P.op("V", lambda E: E.tensor_copy(rowC[:, 0:TT4], pbank[2][0:1, 0:TT4]),
                 reads=[pbank_r[2]], writes=[rowC_r], label=f"hsre{tt}")
            P.op("T", lambda E: E.matmul(
                pbank[2][:, 0:TT4], cn_t["c_ones1"][0:1, :], rowC[:, 0:TT4],
                start=True, stop=True),
                reads=[rowC_r, cn_r["c_ones1"]], writes=[pbank_r[2]], label=f"hsb{tt}")
            P.op("V", lambda E: E.tensor_copy(scbc_t[:, :], pbank[2][:, 0:TT4]),
                 reads=[pbank_r[2]], writes=[scbc_r], label=f"hsbe{tt}")
            if tt == 0:
                tap("xqf", xqf_t[:, :], (128, 8 * TT4), [xqf_r], BF16)
            dma("Y", xrb_t[:, :], s_x2[:, :, ts:ts + TT4], [xrb_r],
                reads=[sr_x2[tt]], label=f"rx2{tt}")
            for j in range(8):
                bank = 3 + (j % 2)
                for k in range(NCH):
                    P.op("T", lambda E, j=j, k=k, bank=bank: E.matmul(
                        pbank[bank][:, 0:TT4],
                        tern_t[:, k * 1024 + j * 128: k * 1024 + j * 128 + 128],
                        xqf_t[:, k * TT4:(k + 1) * TT4],
                        start=(k == 0), stop=(k == NCH - 1)),
                        reads=[tern_r, xqf_r], writes=[pbank_r[bank]], label=f"fm{j}_{k}", acc=(k != 0))
                P.op("V", lambda E, j=j, bank=bank: E.scalar_tensor_tensor(
                    sq2_t[:, :], pbank[bank][:, 0:TT4], col_t[:, j:j + 1],
                    scbc_t[:, :], OP.mult, OP.mult),
                    reads=[pbank_r[bank], col_r[0], scbc_r], writes=[sq2_r],
                    label=f"fe{j}")
                P.op("V", lambda E, j=j: E.tensor_tensor(
                    xrb_t[:, j * TT4:(j + 1) * TT4], xrb_t[:, j * TT4:(j + 1) * TT4],
                    sq2_t[:, :], OP.add),
                    reads=[xrb_r, sq2_r], writes=[xrb_r], label=f"fr{j}")
            if tt == 0:
                tap("ffn0", xrb_t[:, :], (128, 8 * TT4), [xrb_r], BF16)
            emit_ln(TT4, xrb_t, xrb_r, x2t_t, x2t_r, 4, 5, sq2_t, sq2_r)
            dma("Y", s_x3[:, :, ts:ts + TT4], x2t_t[:, :], [sr_x3[tt]],
                reads=[x2t_r],
                label=f"sx3{tt}")
            if tt == 0:
                tap("x3", x2t_t[:, :], (128, 8 * TT4), [x2t_r], BF16)

    if "PC" in phases:
        AB.reset()
        wq1_t, wq1_r = AB.alloc("wq1", 8 * 1024)
        wk1_t, wk1_r = AB.alloc("wk1", 8 * 1024)
        wv1_t, wv1_r = AB.alloc("wv1", 8 * 1024)
        dma("G", wq1_t[:, :], d_wq1[:, :], [wq1_r], label="lwq1")
        dma("G", wk1_t[:, :], d_wk1[:, :], [wk1_r], label="lwk1")
        dma("G", wv1_t[:, :], d_wv1[:, :], [wv1_r], label="lwv1")
        bq_t = col_t[:, 8:9]; bk_t = col_t[:, 9:10]; bv_t = col_t[:, 10:11]
        bq1_t, bq1_r = AFA.alloc("bq1", 8) if False else (None, None)
        bcol_t = sb("bcol", (128, 24), F32); bcol_r = Reg("bcol")
        dma("Y", bcol_t[:, 0:8], d_bq1[:, :], [bcol_r], label="lbq")
        dma("Y", bcol_t[:, 8:16], d_bk1[:, :], [bcol_r], label="lbk")
        dma("Y", bcol_t[:, 16:24], d_bv1[:, :], [bcol_r], label="lbv")
        NQ = T // 128
        RING = 8
        x3r_t, _r = AB.alloc("x3r", RING * 1024)
        x3r_r = [Reg(f"x3r{i}") for i in range(RING)]
        for r in x3r_r:
            r.rs = list(_r.rs)
        bindsplit(_r, x3r_r)
        qr_t, _r = AB.alloc("qring", RING * 1024)
        qr_r = [Reg(f"qr{i}") for i in range(RING)]
        for r in qr_r:
            r.rs = list(_r.rs)
        bindsplit(_r, qr_r)
        kr_t, _r = AB.alloc("kring", RING * 1024)
        kr_r = [Reg(f"kr{i}") for i in range(RING)]
        for r in kr_r:
            r.rs = list(_r.rs)
        bindsplit(_r, kr_r)
        vr_t, _r = AB.alloc("vring", RING * 1024)
        vr_r = [Reg(f"vr{i}") for i in range(RING)]
        for r in vr_r:
            r.rs = list(_r.rs)
        bindsplit(_r, vr_r)
        vTr_t, _r = AB.alloc("vTring", RING * 1024)
        vTr_r = [Reg(f"vT{i}") for i in range(RING)]
        for r in vTr_r:
            r.rs = list(_r.rs)
        bindsplit(_r, vTr_r)
        wat_t, wat_r = AB.alloc("wat", 384)
        waT_t, waT_r = AB.alloc("waT", 384)
        x4b_t, x4b_r = AB.alloc("x4b", 1024)
        AFA.reset()
        sc_t, sc_r = AFA.alloc("scores", 384)
        xrc_t, xrc_r = AFA.alloc("xrc", 1024)
        sqc_t, sqc_r = AFA.alloc("sqc", 128)
        smx_t, smx_r = AFA.alloc("smx", 2)

        def pcproj(cc):
            ri = cc % RING
            dma("G", x3r_t[:, ri * 1024:(ri + 1) * 1024], s_x3[:, :, cc * 128:(cc + 1) * 128],
                [x3r_r[ri]], reads=[sr_x3[cc // 4]], label=f"lx3{cc}")
            for (wt, wr, ring_t, ring_r, bofs) in (
                    (wq1_t, wq1_r, qr_t, qr_r, 0), (wk1_t, wk1_r, kr_t, kr_r, 8),
                    (wv1_t, wv1_r, vr_t, vr_r, 16)):
                for j in range(8):
                    bank = j % 2
                    for k in range(NCH):
                        P.op("T", lambda E, j=j, k=k, bank=bank, wt=wt, ri=ri: E.matmul(
                            pbank[bank][:, 0:128],
                            wt[:, k * 1024 + j * 128: k * 1024 + j * 128 + 128],
                            x3r_t[:, ri * 1024 + k * 128: ri * 1024 + (k + 1) * 128],
                            start=(k == 0), stop=(k == NCH - 1)),
                            reads=[wr, x3r_r[ri]], writes=[pbank_r[bank]],
                            label=f"p1_{cc}_{bofs}_{j}_{k}", acc=(k != 0))
                    P.op("S", lambda E, j=j, bank=bank, ring_t=ring_t, ri=ri, bofs=bofs:
                         E.activation(ring_t[:, ri * 1024 + j * 128: ri * 1024 + (j + 1) * 128],
                                      pbank[bank][:, 0:128], AF.Identity,
                                      bias=bcol_t[:, bofs + j:bofs + j + 1]),
                         reads=[pbank_r[bank], bcol_r], writes=[ring_r[ri]],
                         label=f"p1e_{cc}_{bofs}_{j}")
            for k in range(8):
                pb_i = k % 2
                P.op("T", lambda E, k=k, ri=ri, pb_i=pb_i: E.transpose(
                    pbf[pb_i][:, (k // 2) * 128:(k // 2) * 128 + 128],
                    vr_t[:, ri * 1024 + k * 128: ri * 1024 + (k + 1) * 128], idb[:, :]),
                    reads=[vr_r[ri], idbr], writes=[pbf_r[pb_i]], label=f"vt1_{cc}_{k}")
                P.op("V", lambda E, k=k, ri=ri, pb_i=pb_i: E.tensor_copy(
                    vTr_t[:, ri * 1024 + k * 128: ri * 1024 + (k + 1) * 128],
                    pbf[pb_i][:, (k // 2) * 128:(k // 2) * 128 + 128]),
                    reads=[pbf_r[pb_i]], writes=[vTr_r[ri]], label=f"vt1e_{cc}_{k}")

        def pcattn(c):
            kcs = [kc for kc in (c - 1, c, c + 1) if 0 <= kc < NQ]
            mk = "mid" if len(kcs) == 3 else ("first" if c == 0 else "last")
            W = 128 * len(kcs)
            for ik, kc in enumerate(kcs):
                ri = kc % RING
                for k in range(NCH):
                    P.op("T", lambda E, ik=ik, ri=ri, k=k, c=c: E.matmul(
                        pbank[5][:, ik * 128:(ik + 1) * 128],
                        qr_t[:, (c % RING) * 1024 + k * 128: (c % RING) * 1024 + (k + 1) * 128],
                        kr_t[:, ri * 1024 + k * 128: ri * 1024 + (k + 1) * 128],
                        start=(k == 0), stop=(k == NCH - 1)),
                        reads=[qr_r[c % RING], kr_r[ri]], writes=[pbank_r[5]],
                        label=f"sc{c}_{ik}_{k}", acc=(k != 0))
            P.op("V", lambda E, W=W, mk=mk: E.scalar_tensor_tensor(
                sc_t[:, 0:W], pbank[5][:, 0:W], 1.0 / 32.0, masks_t[mk][:, 0:W],
                OP.mult, OP.add),
                reads=[pbank_r[5], masks_r[mk]], writes=[sc_r], label=f"msk{c}")
            P.op("V", lambda E, W=W: E.reduce_max(
                smx_t[:, 0:1], sc_t[:, 0:W], AX.X, negate=True),
                reads=[sc_r], writes=[smx_r], label=f"mx{c}")
            P.op("S", lambda E, W=W: E.activation(
                sc_t[:, 0:W], sc_t[:, 0:W], AF.Exp, bias=smx_t[:, 0:1]),
                reads=[sc_r, smx_r], writes=[sc_r], label=f"ex{c}")
            P.op("V", lambda E, W=W: E.reduce_sum(smx_t[:, 1:2], sc_t[:, 0:W], AX.X),
                 reads=[sc_r], writes=[smx_r], label=f"sm{c}")
            P.op("V", lambda E: E.reciprocal(smx_t[:, 1:2], smx_t[:, 1:2]),
                 reads=[smx_r], writes=[smx_r], label=f"rc{c}")
            P.op("V", lambda E, W=W: E.tensor_scalar(
                wat_t[:, 0:W], sc_t[:, 0:W], smx_t[:, 1:2], None, OP.mult),
                reads=[sc_r, smx_r], writes=[wat_r], label=f"wn{c}")
            for ik in range(len(kcs)):
                pb_i = ik % 2
                P.op("T", lambda E, ik=ik, pb_i=pb_i: E.transpose(
                    pbf[pb_i][:, 0:128] if False else pbf[pb_i][:, ik * 128:ik * 128 + 128],
                    wat_t[:, ik * 128:(ik + 1) * 128], idb[:, :]),
                    reads=[wat_r, idbr], writes=[pbf_r[pb_i]], label=f"wt{c}_{ik}")
                P.op("V", lambda E, ik=ik, pb_i=pb_i: E.tensor_copy(
                    waT_t[:, ik * 128:(ik + 1) * 128],
                    pbf[pb_i][:, ik * 128:ik * 128 + 128]),
                    reads=[pbf_r[pb_i]], writes=[waT_r], label=f"wte{c}_{ik}")
            dma("G", xrc_t[:, :], s_x3[:, :, c * 128:(c + 1) * 128], [xrc_r],
                reads=[sr_x3[c // 4]], label=f"lr{c}")
            for j in range(8):
                bank = j % 2
                for ik, kc in enumerate(kcs):
                    ri = kc % RING
                    P.op("T", lambda E, j=j, ik=ik, ri=ri, bank=bank: E.matmul(
                        pbank[bank][:, 0:128],
                        vTr_t[:, ri * 1024 + j * 128: ri * 1024 + (j + 1) * 128],
                        waT_t[:, ik * 128:(ik + 1) * 128],
                        start=(ik == 0), stop=(ik == len(kcs) - 1)),
                        reads=[vTr_r[ri], waT_r], writes=[pbank_r[bank]],
                        label=f"av{c}_{j}_{ik}", acc=(ik != 0))
                P.op("V", lambda E, j=j, bank=bank: E.scalar_tensor_tensor(
                    xrc_t[:, j * 128:(j + 1) * 128], pbank[bank][:, 0:128],
                    1.0, xrc_t[:, j * 128:(j + 1) * 128], OP.mult, OP.add),
                    reads=[pbank_r[bank], xrc_r], writes=[xrc_r], label=f"ave{c}_{j}")
            emit_ln(128, xrc_t, xrc_r, x4b_t, x4b_r, 4, 5, sqc_t, sqc_r)
            dma("Y", s_x4[:, :, c * 128:(c + 1) * 128], x4b_t[:, :], [sr_x4[c]],
                reads=[x4b_r],
                label=f"sx4{c}")
            if c == 0:
                tap("x4", x4b_t[:, :], (128, 1024), [x4b_r], BF16)

        pcproj(0)
        pcproj(1)
        for c in range(NQ):
            if c + 2 < NQ:
                pcproj(c + 2)
            pcattn(c)

    if "PD" in phases:
        AB.reset()
        AFA.reset()
        wfp_t, _r = AFA.alloc("wfp", 2 * 2048)
        wfp_r = [Reg("wfpA"), Reg("wfpB")]
        for r in wfp_r:
            r.rs = list(_r.rs)
        bindsplit(_r, wfp_r)
        acc_t, acc_r = AFA.alloc("acc", 4)
        scl_t, scl_r = AFA.alloc("scl", 8)
        tpc_t, _r = AB.alloc("tpc", 2 * 2048)
        tpc_r = [Reg("tpcA"), Reg("tpcB")]
        for r in tpc_r:
            r.rs = list(_r.rs)
        bindsplit(_r, tpc_r)
        def _pgu(dscr, p):
            return dscr[:, p // 2, (p % 2) * 2048:(p % 2) * 2048 + 2048]

        def _pd(dscr, p):
            return dscr[:, 2 * p:2 * p + 2, :]

        WSPECS = [(d_wg1, s_tg, 8 * 4096, 0, _pgu), (d_wu1, s_tu, 8 * 4096, 1, _pgu),
                  (d_wd1, s_td, 32 * 1024, 2, _pd)]
        for wi, (dsrc, dscr, ncols, ai, pslice) in enumerate(WSPECS):
            npc = ncols // 2048
            P.op("V", lambda E, ai=ai: E.memset(acc_t[:, ai:ai + 1], 0.0),
                 writes=[acc_r], label=f"qz{wi}")
            for p in range(npc):
                pi = p % 2
                dma("Y", wfp_t[:, pi * 2048:(pi + 1) * 2048],
                    dsrc[:, p * 2048:(p + 1) * 2048], [wfp_r[pi]], label=f"qp{wi}_{p}")
                P.op("V", lambda E, pi=pi: E.tensor_reduce(
                    scl_t[:, 4:5], wfp_t[:, pi * 2048:(pi + 1) * 2048], AX.X, OP.add,
                    apply_absolute_value=True),
                    reads=[wfp_r[pi]], writes=[scl_r], label=f"qr{wi}_{p}")
                P.op("V", lambda E, ai=ai: E.tensor_tensor(
                    acc_t[:, ai:ai + 1], acc_t[:, ai:ai + 1], scl_t[:, 4:5], OP.add),
                    reads=[acc_r, scl_r], writes=[acc_r], label=f"qa{wi}_{p}")
            P.op("T", lambda E, ai=ai: E.matmul(
                pbank[2][0:1, 0:1], acc_t[:, ai:ai + 1], cn_t["c_ones128"][:, 0:1],
                start=True, stop=True),
                reads=[acc_r, cn_r["c_ones128"]], writes=[pbank_r[2]], label=f"qt{wi}")
            P.op("V", lambda E, ncols=ncols: E.tensor_scalar(
                scl_t[0:1, 0:1], pbank[2][0:1, 0:1], 1.0 / (ncols * 128.0), 1e-5,
                OP.mult, OP.max),
                reads=[pbank_r[2]], writes=[scl_r], label=f"qm{wi}")
            P.op("T", lambda E: E.matmul(
                pbank[2][:, 4:5], cn_t["c_ones1"][0:1, :], scl_t[0:1, 0:1],
                start=True, stop=True),
                reads=[scl_r, cn_r["c_ones1"]], writes=[pbank_r[2]], label=f"qb{wi}")
            P.op("V", lambda E, wi=wi: E.tensor_copy(col_t[:, 11 + wi:12 + wi],
                                                     pbank[2][:, 4:5]),
                 reads=[pbank_r[2]], writes=[col_r[1]], label=f"qbe{wi}")
            for p in range(npc):
                pi = p % 2
                dma("Y", wfp_t[:, pi * 2048:(pi + 1) * 2048],
                    dsrc[:, p * 2048:(p + 1) * 2048], [wfp_r[pi]], label=f"q2_{wi}_{p}")
                if p == 0:
                    P.op("V", lambda E, wi=wi: E.reciprocal(
                        col_t[:, 14:15], col_t[:, 11 + wi:12 + wi]),
                        reads=[col_r[1]], writes=[col_r[2]], label=f"qr_{wi}")
                P.op("V", lambda E, pi=pi: E.tensor_scalar(
                    wfp_t[:, pi * 2048:(pi + 1) * 2048],
                    wfp_t[:, pi * 2048:(pi + 1) * 2048],
                    col_t[:, 14:15], None, OP.mult),
                    reads=[wfp_r[pi], col_r[2]], writes=[wfp_r[pi]], label=f"q3_{wi}_{p}")
                P.op("V", lambda E, pi=pi: E.tensor_scalar(
                    tpc_t[:, pi * 2048:(pi + 1) * 2048],
                    wfp_t[:, pi * 2048:(pi + 1) * 2048], M52, -M52, OP.add, OP.add),
                    reads=[wfp_r[pi]], writes=[tpc_r[pi]], label=f"q4_{wi}_{p}")
                P.op("V", lambda E, pi=pi: E.tensor_scalar(
                    wfp_t[:, pi * 2048:(pi + 1) * 2048],
                    wfp_t[:, pi * 2048:(pi + 1) * 2048], M23, -M23, OP.add, OP.add),
                    reads=[wfp_r[pi]], writes=[wfp_r[pi]], label=f"q5_{wi}_{p}")
                P.op("V", lambda E, pi=pi: E.scalar_tensor_tensor(
                    wfp_t[:, pi * 2048:(pi + 1) * 2048],
                    wfp_t[:, pi * 2048:(pi + 1) * 2048], selc_t[:, 1:2],
                    tpc_t[:, pi * 2048:(pi + 1) * 2048], OP.mult, OP.add),
                    reads=[wfp_r[pi], selc_r, tpc_r[pi]], writes=[wfp_r[pi]],
                    label=f"q5b_{wi}_{p}")
                P.op("V", lambda E, pi=pi: E.tensor_scalar(
                    tpc_t[:, pi * 2048:(pi + 1) * 2048],
                    wfp_t[:, pi * 2048:(pi + 1) * 2048], -1.0, 1.0, OP.max, OP.min),
                    reads=[wfp_r[pi]], writes=[tpc_r[pi]], label=f"q6_{wi}_{p}")
                dma("Y", pslice(dscr, p),
                    tpc_t[:, pi * 2048:(pi + 1) * 2048], [sr_w[ai][p]],
                    reads=[tpc_r[pi]], label=f"q7_{wi}_{p}")

        AB.reset()
        TD = 256
        NTD = T // TD
        tg_t, _r = AB.alloc("tgp", 2 * 4096)
        tg_r = [Reg("tgA"), Reg("tgB")]
        for r in tg_r:
            r.rs = list(_r.rs)
        bindsplit(_r, tg_r)
        tu_t, _r = AB.alloc("tup", 2 * 4096)
        tu_r = [Reg("tuA"), Reg("tuB")]
        for r in tu_r:
            r.rs = list(_r.rs)
        bindsplit(_r, tu_r)
        td_t, _r = AB.alloc("tdp", 2 * 4096)
        td_r = [Reg("tdA"), Reg("tdB")]
        for r in td_r:
            r.rs = list(_r.rs)
        bindsplit(_r, td_r)
        x4t_t, x4t_r = AB.alloc("x4t", 8 * TD)
        xq4_t, xq4_r = AB.alloc("xq4", 8 * TD)
        g1_t, _r = AB.alloc("g1", 32 * TD)
        g1_r = [Reg(f"g1_{j}") for j in range(32)]
        for r in g1_r:
            r.rs = list(_r.rs)
        bindsplit(_r, g1_r)
        u1_t, _r = AB.alloc("u1", 32 * TD)
        u1_r = [Reg(f"u1_{j}") for j in range(32)]
        for r in u1_r:
            r.rs = list(_r.rs)
        bindsplit(_r, u1_r)
        htk_t, htk_r = AB.alloc("htk", 4096)
        AFA.reset()
        xf4_t, xf4_r = AFA.alloc("xf4", 1024)
        mq4_t, mq4_r = AFA.alloc("mq4", 8)
        srow_t, srow_r = AFA.alloc("srow", 3 * TD)
        sbc_t, sbc_r = AFA.alloc("sbc", 2 * TD)
        xrd_t, xrd_r = AFA.alloc("xrd", 8 * TD)
        sqd_t, sqd_r = AFA.alloc("sqd", TD)
        yb_t, yb_r = AB.alloc("yb", 8 * TD)

        def quant_tokmajor(src_t, src_r, nfeat, dst_t, dst_r, tcbase, qmax, scol_ofs,
                           srow_ofs, tag):
            nblk = nfeat // 128
            for k in range(nblk):
                pb_i = k % 2
                P.op("T", lambda E, k=k, pb_i=pb_i: E.transpose(
                    pbf[pb_i][:, (k % 8) // 2 * 128:(k % 8) // 2 * 128 + 128],
                    src_t[:, k * TD + tcbase: k * TD + tcbase + 128], idb[:, :]),
                    reads=[src_r[k] if isinstance(src_r, list) else src_r, idbr],
                    writes=[pbf_r[pb_i]], label=f"{tag}T{k}")
                P.op("V", lambda E, k=k, pb_i=pb_i: E.tensor_copy(
                    htk_t[:, k * 128:(k + 1) * 128],
                    pbf[pb_i][:, (k % 8) // 2 * 128:(k % 8) // 2 * 128 + 128]),
                    reads=[pbf_r[pb_i]], writes=[htk_r], label=f"{tag}Te{k}")
            P.op("V", lambda E, nfeat=nfeat: E.tensor_reduce(
                mq4_t[:, 0:1], htk_t[:, 0:nfeat], AX.X, OP.max,
                apply_absolute_value=True),
                reads=[htk_r], writes=[mq4_r], label=f"{tag}m")
            P.op("V", lambda E: E.tensor_scalar(
                mq4_t[:, 1:2], mq4_t[:, 0:1], 1e-5, None, OP.max),
                reads=[mq4_r], writes=[mq4_r], label=f"{tag}mc")
            P.op("V", lambda E, nfeat=nfeat: E.tensor_scalar(
                xf4_t[:, 0:nfeat] if nfeat <= 1024 else None,
                htk_t[:, 0:nfeat], 127.0, None, OP.mult),
                reads=[htk_r], writes=[xf4_r], label=f"{tag}v0") if nfeat <= 1024 else None
            if nfeat <= 1024:
                P.op("V", lambda E: E.reciprocal(mq4_t[:, 3:4], mq4_t[:, 1:2]),
                     reads=[mq4_r], writes=[mq4_r], label=f"{tag}vr")
                P.op("V", lambda E, nfeat=nfeat: E.tensor_scalar(
                    xf4_t[:, 0:nfeat], xf4_t[:, 0:nfeat], mq4_t[:, 3:4], None,
                    OP.mult),
                    reads=[xf4_r, mq4_r], writes=[xf4_r], label=f"{tag}v1")
                P.op("V", lambda E, nfeat=nfeat: E.tensor_scalar(
                    htk_t[:, 0:nfeat], xf4_t[:, 0:nfeat], M52, -M52, OP.add, OP.add),
                    reads=[xf4_r], writes=[htk_r], label=f"{tag}v2")
                P.op("V", lambda E, nfeat=nfeat: E.tensor_scalar(
                    xf4_t[:, 0:nfeat], xf4_t[:, 0:nfeat], M23, -M23, OP.add, OP.add),
                    reads=[xf4_r], writes=[xf4_r], label=f"{tag}v3")
                P.op("V", lambda E, nfeat=nfeat: E.scalar_tensor_tensor(
                    xf4_t[:, 0:nfeat], xf4_t[:, 0:nfeat], selc_t[:, 1:2],
                    htk_t[:, 0:nfeat], OP.mult, OP.add),
                    reads=[xf4_r, selc_r, htk_r], writes=[xf4_r], label=f"{tag}v3b")
                P.op("V", lambda E, nfeat=nfeat: E.tensor_scalar(
                    htk_t[:, 0:nfeat], xf4_t[:, 0:nfeat], -128.0, 127.0, OP.max, OP.min),
                    reads=[xf4_r], writes=[htk_r], label=f"{tag}v4")
            else:
                for hb in range(nfeat // 1024):
                    sl = slice(hb * 1024, (hb + 1) * 1024)
                    P.op("V", lambda E, sl=sl: E.tensor_scalar(
                        xf4_t[:, 0:1024], htk_t[:, sl], 127.0, None, OP.mult),
                        reads=[htk_r], writes=[xf4_r], label=f"{tag}w0_{sl.start}")
                    if sl.start == 0:
                        P.op("V", lambda E: E.reciprocal(mq4_t[:, 3:4], mq4_t[:, 1:2]),
                             reads=[mq4_r], writes=[mq4_r], label=f"{tag}wr")
                    P.op("V", lambda E, sl=sl: E.tensor_scalar(
                        xf4_t[:, 0:1024], xf4_t[:, 0:1024], mq4_t[:, 3:4], None,
                        OP.mult),
                        reads=[xf4_r, mq4_r], writes=[xf4_r], label=f"{tag}w1_{sl.start}")
                    P.op("V", lambda E, sl=sl: E.tensor_scalar(
                        htk_t[:, sl], xf4_t[:, 0:1024], M52, -M52, OP.add, OP.add),
                        reads=[xf4_r], writes=[htk_r], label=f"{tag}w2_{sl.start}")
                    P.op("V", lambda E, sl=sl: E.tensor_scalar(
                        xf4_t[:, 0:1024], xf4_t[:, 0:1024], M23, -M23, OP.add, OP.add),
                        reads=[xf4_r], writes=[xf4_r], label=f"{tag}w3_{sl.start}")
                    P.op("V", lambda E, sl=sl: E.scalar_tensor_tensor(
                        xf4_t[:, 0:1024], xf4_t[:, 0:1024], selc_t[:, 1:2],
                        htk_t[:, sl], OP.mult, OP.add),
                        reads=[xf4_r, selc_r, htk_r], writes=[xf4_r],
                        label=f"{tag}w3b_{sl.start}")
                    P.op("V", lambda E, sl=sl: E.tensor_scalar(
                        htk_t[:, sl], xf4_t[:, 0:1024], -128.0, 127.0, OP.max, OP.min),
                        reads=[xf4_r], writes=[htk_r], label=f"{tag}w4_{sl.start}")
            P.op("V", lambda E: E.tensor_scalar(
                mq4_t[:, 2:3], mq4_t[:, 1:2], 1.0 / 127.0, None, OP.mult),
                reads=[mq4_r], writes=[mq4_r], label=f"{tag}sr0")
            P.op("T", lambda E: E.matmul(
                pbank[2][0:1, 0:128], mq4_t[:, 2:3], cn_t["c_ident32"][:, :],
                start=True, stop=True),
                reads=[mq4_r, cn_r["c_ident32"]], writes=[pbank_r[2]], label=f"{tag}sr1")
            P.op("V", lambda E: E.tensor_copy(
                srow_t[0:1, srow_ofs + tcbase: srow_ofs + tcbase + 128],
                pbank[2][0:1, 0:128]),
                reads=[pbank_r[2]], writes=[srow_r], label=f"{tag}sr2")
            for k in range(nblk):
                pb_i = k % 2
                P.op("T", lambda E, k=k, pb_i=pb_i: E.transpose(
                    pbf[pb_i][:, ((k % 8) // 2 + 4) * 128:((k % 8) // 2 + 4) * 128 + 128],
                    htk_t[:, k * 128:(k + 1) * 128], idb[:, :]),
                    reads=[htk_r, idbr], writes=[pbf_r[pb_i]], label=f"{tag}B{k}")
                P.op("V", lambda E, k=k, pb_i=pb_i: E.tensor_copy(
                    dst_t[:, k * TD + tcbase: k * TD + tcbase + 128],
                    pbf[pb_i][:, ((k % 8) // 2 + 4) * 128:((k % 8) // 2 + 4) * 128 + 128]),
                    reads=[pbf_r[pb_i]],
                    writes=[dst_r[k] if isinstance(dst_r, list) else dst_r],
                    label=f"{tag}Be{k}")

        for td in range(NTD):
            ts = td * TD
            dma("Y", x4t_t[:, :], s_x4[:, :, ts:ts + TD], [x4t_r],
                reads=[sr_x4[2 * td], sr_x4[2 * td + 1]], label=f"lx4{td}")
            for sub in range(TD // 128):
                quant_tokmajor(x4t_t, x4t_r, 1024, xq4_t, xq4_r, sub * 128, 127.0,
                               0, 0, f"xq{td}_{sub}")
            P.op("T", lambda E: E.matmul(
                pbank[2][:, 0:TD], cn_t["c_ones1"][0:1, :], srow_t[0:1, 0:TD],
                start=True, stop=True),
                reads=[srow_r, cn_r["c_ones1"]], writes=[pbank_r[2]], label=f"sxb{td}")
            P.op("V", lambda E: E.tensor_copy(sbc_t[:, 0:TD], pbank[2][:, 0:TD]),
                 reads=[pbank_r[2]], writes=[sbc_r], label=f"sxbe{td}")
            for og in range(8):
                pi = og % 2
                half = og // 4
                dma("Y", tg_t[:, pi * 4096:(pi + 1) * 4096],
                    s_tg[:, :, og * 512:(og + 1) * 512], [tg_r[pi]],
                    reads=[sr_w[0][p] for p in range(16) if p % 2 == half],
                    label=f"ltg{td}_{og}")
                dma("Y", tu_t[:, pi * 4096:(pi + 1) * 4096],
                    s_tu[:, :, og * 512:(og + 1) * 512], [tu_r[pi]],
                    reads=[sr_w[1][p] for p in range(16) if p % 2 == half],
                    label=f"ltu{td}_{og}")
                for oc in range(4):
                    j = og * 4 + oc
                    bank = oc % 2
                    for k in range(NCH):
                        P.op("T", lambda E, k=k, oc=oc, pi=pi, bank=bank: E.matmul(
                            pbank[bank][:, 0:TD],
                            tg_t[:, pi * 4096 + k * 512 + oc * 128:
                                 pi * 4096 + k * 512 + oc * 128 + 128],
                            xq4_t[:, k * TD:(k + 1) * TD],
                            start=(k == 0), stop=(k == NCH - 1)),
                            reads=[tg_r[pi], xq4_r], writes=[pbank_r[bank]],
                            label=f"gm{td}_{j}_{k}", acc=(k != 0))
                    P.op("V", lambda E, bank=bank: E.scalar_tensor_tensor(
                        xf4_t[:, 0:TD], pbank[bank][:, 0:TD], col_t[:, 11:12],
                        sbc_t[:, 0:TD], OP.mult, OP.mult),
                        reads=[pbank_r[bank], col_r[1], sbc_r], writes=[xf4_r],
                        label=f"ge{td}_{j}")
                    P.op("S", lambda E, j=j: E.activation(
                        g1_t[:, j * TD:(j + 1) * TD], xf4_t[:, 0:TD], AF.Silu),
                        reads=[xf4_r], writes=[g1_r[j]], label=f"gs{td}_{j}")
                    bank2 = 2 + (oc % 2)
                    for k in range(NCH):
                        P.op("T", lambda E, k=k, oc=oc, pi=pi, bank2=bank2: E.matmul(
                            pbank[bank2][:, 0:TD],
                            tu_t[:, pi * 4096 + k * 512 + oc * 128:
                                 pi * 4096 + k * 512 + oc * 128 + 128],
                            xq4_t[:, k * TD:(k + 1) * TD],
                            start=(k == 0), stop=(k == NCH - 1)),
                            reads=[tu_r[pi], xq4_r], writes=[pbank_r[bank2]],
                            label=f"um{td}_{j}_{k}", acc=(k != 0))
                    P.op("V", lambda E, j=j, bank2=bank2: E.scalar_tensor_tensor(
                        u1_t[:, j * TD:(j + 1) * TD], pbank[bank2][:, 0:TD],
                        col_t[:, 12:13], sbc_t[:, 0:TD], OP.mult, OP.mult),
                        reads=[pbank_r[bank2], col_r[1], sbc_r], writes=[u1_r[j]],
                        label=f"ue{td}_{j}")
                    P.op("V", lambda E, j=j: E.tensor_tensor(
                        u1_t[:, j * TD:(j + 1) * TD], g1_t[:, j * TD:(j + 1) * TD],
                        u1_t[:, j * TD:(j + 1) * TD], OP.mult),
                        reads=[g1_r[j], u1_r[j]], writes=[u1_r[j]], label=f"h{td}_{j}")
            for sub in range(TD // 128):
                quant_tokmajor(u1_t, u1_r, 4096, u1_t, u1_r, sub * 128, 127.0,
                               0, TD, f"hq{td}_{sub}")
            P.op("V", lambda E: E.tensor_scalar(
                srow_t[0:1, TD:2 * TD], srow_t[0:1, TD:2 * TD], col_t[0:1, 13:14],
                None, OP.mult),
                reads=[srow_r, col_r[1]], writes=[srow_r], label=f"hr{td}")
            P.op("T", lambda E: E.matmul(
                pbank[2][:, 0:TD], cn_t["c_ones1"][0:1, :], srow_t[0:1, TD:2 * TD],
                start=True, stop=True),
                reads=[srow_r, cn_r["c_ones1"]], writes=[pbank_r[2]], label=f"hb{td}")
            P.op("V", lambda E: E.tensor_copy(sbc_t[:, TD:2 * TD], pbank[2][:, 0:TD]),
                 reads=[pbank_r[2]], writes=[sbc_r], label=f"hbe{td}")
            dma("G", xrd_t[:, :], s_x4[:, :, ts:ts + TD], [xrd_r],
                reads=[sr_x4[2 * td], sr_x4[2 * td + 1]], label=f"lr4{td}")
            for j in range(8):
                pi = j % 2
                dma("Y", td_t[:, pi * 4096:(pi + 1) * 4096],
                    s_td[:, :, j * 128:(j + 1) * 128], [td_r[pi]],
                    reads=sr_w[2], label=f"ltd{td}_{j}")
                bank = 3 + (j % 2)
                for k in range(32):
                    P.op("T", lambda E, k=k, pi=pi, bank=bank: E.matmul(
                        pbank[bank][:, 0:TD],
                        td_t[:, pi * 4096 + k * 128: pi * 4096 + (k + 1) * 128],
                        u1_t[:, k * TD:(k + 1) * TD],
                        start=(k == 0), stop=(k == 31)),
                        reads=[td_r[pi], u1_r[k]], writes=[pbank_r[bank]],
                        label=f"dm{td}_{j}_{k}", acc=(k != 0))
                P.op("V", lambda E, j=j, bank=bank: E.tensor_tensor(
                    sqd_t[:, 0:TD], pbank[bank][:, 0:TD], sbc_t[:, TD:2 * TD], OP.mult),
                    reads=[pbank_r[bank], sbc_r], writes=[sqd_r], label=f"de{td}_{j}")
                P.op("V", lambda E, j=j: E.tensor_tensor(
                    xrd_t[:, j * TD:(j + 1) * TD], xrd_t[:, j * TD:(j + 1) * TD],
                    sqd_t[:, 0:TD], OP.add),
                    reads=[xrd_r, sqd_r], writes=[xrd_r], label=f"dr{td}_{j}")
            emit_ln(TD, xrd_t, xrd_r, yb_t, yb_r, 4, 5, sqd_t, sqd_r,
                    extra_out=None)
            dma("G", yout[:, :, ts:ts + TD], yb_t[:, :], [], reads=[yb_r],
                label=f"out{td}")

    if "PD" not in phases:
        AFA.reset()
        cp_t, cp_r = AFA.alloc("cp", 2048)
        for k in range(NCH):
            for half in range(T // 2048):
                dma("Y", cp_t[:, 0:2048], xin[:, k, half * 2048:half * 2048 + 2048],
                    [cp_r], label=f"cp{k}")
                dma("Y", yout[:, k, half * 2048:half * 2048 + 2048], cp_t[:, 0:2048],
                    [], reads=[cp_r], label=f"cpo{k}")

    def lane_finals(e):
        n, R = P.ndma[e], LANES[e]
        return [(l, 16 * (n // R + (1 if l < n % R else 0))) for l in range(R)
                if n // R + (1 if l < n % R else 0) > 0]

    def mk_fin(e):
        fins = lane_finals(e)
        def fn(E):
            inst = None
            for l, v in fins:
                inst = E.wait_ge(SEMS_DMA[e][l], v)
            return inst
        return fn
    P.op("G", mk_fin("Y"), label="finy")
    if P.ndma["G"]:
        P.op("V", mk_fin("G"), label="fing")

    SEMS_DMA = {}
    import contextlib as _ctx
    with _ctx.ExitStack() as es:
        sems_exec = {e: es.enter_context(nc.semaphore(f"sx_{e}")) for e in ENGS}
        sems_dma = {e: [es.enter_context(nc.semaphore(f"sd_{e}{l}"))
                        for l in range(LANES[e])] for e in ENGS}
        SEMS_DMA.update(sems_dma)
        block = es.enter_context(nc.Block())
        P.emit(nc, block, sems_exec, sems_dma)
    ctx.close()
    return nc, taps



def prep_inputs(inputs, core, T=2048):
    x = np.asarray(inputs["L0_attn_hidden_states"])[core]
    d = {}
    d["x"] = np.ascontiguousarray(
        x.T.reshape(8, 128, T).transpose(1, 0, 2)).astype(np.float32, copy=False)
    d["nw"] = col_chunks(np.asarray(inputs["L0_attn_norm_w"]))
    d["qw"] = chunk_lhsT(np.asarray(inputs["L0_attn_q_w"]))
    d["kw"] = chunk_lhsT(np.asarray(inputs["L0_attn_k_w"]))
    d["vw"] = chunk_lhsT(np.asarray(inputs["L0_attn_v_w"]))
    aw = np.asarray(inputs["L0_attn_a_w"]); bw = np.asarray(inputs["L0_attn_b_w"])
    d["abw"] = chunk_lhsT(np.concatenate(
        [aw, np.zeros((16, 1024), np.float32), bw], axis=0))
    d["gw"] = chunk_lhsT(np.asarray(inputs["L0_attn_g_w"]))
    d["opw"] = chunk_lhsT(np.asarray(inputs["L0_attn_o_proj_w"]))
    d["outw"] = chunk_lhsT(np.asarray(inputs["L0_attn_out_proj_w"]))
    for nm, key in (("convq", "L0_attn_conv_q_w"), ("convk", "L0_attn_conv_k_w"),
                    ("convv", "L0_attn_conv_v_w")):
        w = np.asarray(inputs[key])
        d[nm] = np.ascontiguousarray(
            w.reshape(-1, 128, 4).transpose(1, 0, 2).reshape(128, -1)).astype(np.float32)
    d["dtb"] = np.asarray(inputs["L0_attn_dt_bias"]).reshape(16, 1).astype(np.float32)
    d["alog"] = np.asarray(inputs["L0_attn_A_log"]).reshape(16, 1).astype(np.float32)
    d["onw"] = np.asarray(inputs["L0_attn_o_norm_w"]).reshape(128, 1).astype(np.float32)
    d["ffw"] = chunk_lhsT(np.asarray(inputs["L0_ffn_weight"]))
    h = np.array([[1.0]], dtype=np.float64)
    while h.shape[0] < 1024:
        h = np.block([[h, h], [h, -h]])
    d["hpm"] = chunk_rhs(h.astype(np.float32)).astype(BFNP)
    d["wq1"] = chunk_rhs(np.asarray(inputs["L1_attn_Wq"]))
    d["wk1"] = chunk_rhs(np.asarray(inputs["L1_attn_Wk"]))
    d["wv1"] = chunk_rhs(np.asarray(inputs["L1_attn_Wv"]))
    d["bq1"] = col_chunks(np.asarray(inputs["L1_attn_bq"]))
    d["bk1"] = col_chunks(np.asarray(inputs["L1_attn_bk"]))
    d["bv1"] = col_chunks(np.asarray(inputs["L1_attn_bv"]))
    d["wg1"] = chunk_lhsT(np.asarray(inputs["L1_ffn_w_gate"]))
    d["wu1"] = chunk_lhsT(np.asarray(inputs["L1_ffn_w_up"]))
    d["wd1"] = chunk_lhsT(np.asarray(inputs["L1_ffn_w_down"]))
    qi = np.arange(128)
    d["maskmid"] = np.where(
        np.abs((np.arange(384)[None, :] - 128) - qi[:, None]) <= 128, 0.0,
        -1e30).astype(np.float32)
    d["maskfirst"] = np.where(
        np.abs(np.arange(256)[None, :] - qi[:, None]) <= 128, 0.0,
        -1e30).astype(np.float32)
    d["masklast"] = np.where(
        np.abs((np.arange(256)[None, :] - 128) - qi[:, None]) <= 128, 0.0,
        -1e30).astype(np.float32)
    for k, v in host_consts().items():
        d[k] = np.ascontiguousarray(v)
    return d


def kernel(**inputs):
    global LAST_TAPS
    T = 2048
    nc, taps = build(T, tuple(TAPS))
    in_maps = [prep_inputs(inputs, c, T) for c in range(8)]
    res = run_bass_kernel_spmd(nc, in_maps, core_ids=list(range(8)))
    LAST_TAPS = {}
    for name in taps:
        LAST_TAPS[name] = [r["tap_" + name] for r in res.results]
    outs = []
    for c in range(8):
        y = np.asarray(res.results[c]["y"])
        y = y.transpose(1, 0, 2).reshape(1024, T)
        outs.append(y.T)
    return np.stack(outs, axis=0).astype(np.float32)
